# version 03 (final)
# baseline (speedup 1.0000x reference)
import numpy as np
from contextlib import ExitStack

import concourse.bass as bass
import concourse.mybir as mybir
from concourse.bass_utils import run_bass_kernel_spmd

F32 = mybir.dt.float32
BF16 = mybir.dt.bfloat16
AF = mybir.ActivationFunctionType
ALU = mybir.AluOpType
AX = mybir.AxisListType.X

NCORES = 8
B_LOC = 2
N = 2048
D = 1024
NEG = -3.4028235e38

ENGINES = ("tensor", "vector", "scalar", "gpsimd", "sync")


class Prog:
    def __init__(self, nc, stack):
        self.nc = nc
        self.stack = stack
        self.sems = {e: stack.enter_context(nc.semaphore("sem_" + e)) for e in ENGINES}
        self.cnt = {e: 0 for e in ENGINES}
        self.queue = {e: [] for e in ENGINES}
        self.last_w = {}
        self.last_r = {}
        self.waited = {e: {f: 0 for f in ENGINES} for e in ENGINES}

    def _key(self, buf):
        return buf if isinstance(buf, str) else buf.name

    def op(self, engine, fn, reads=(), writes=(), small=False, dma=False):
        waits = []
        drain = False
        inc = 16 if dma else 1
        my_next = self.cnt[engine] + inc

        def need(eng2, val):
            nonlocal drain
            if val <= 0:
                return
            if eng2 == engine:
                return
            if self.waited[engine][eng2] < val:
                waits.append((eng2, val))
                self.waited[engine][eng2] = val

        for b in reads:
            k = self._key(b)
            lw = self.last_w.get(k)
            if lw is not None:
                we, wv, wsmall = lw
                if we == engine:
                    if wsmall:
                        drain = True
                else:
                    need(we, wv)
        for b in writes:
            k = self._key(b)
            lw = self.last_w.get(k)
            if lw is not None:
                we, wv, wsmall = lw
                if we != engine:
                    need(we, wv)
                elif wsmall:
                    drain = True
            for re_, rv in self.last_r.get(k, {}).items():
                if re_ != engine:
                    need(re_, rv)
        for b in reads:
            k = self._key(b)
            self.last_r.setdefault(k, {})[engine] = my_next
        for b in writes:
            k = self._key(b)
            self.last_w[k] = (engine, my_next, small)
        self.cnt[engine] = my_next
        self.queue[engine].append((list(waits), drain, fn, inc))

    def barrier(self):
        snap = dict(self.cnt)
        for e in ENGINES:
            self.wait_all(e, snap)

    def barrier_value(self):
        return dict(self.cnt)

    def wait_all(self, engine, snap):
        for f in ENGINES:
            if f != engine and snap[f] > 0 and self.waited[engine][f] < snap[f]:
                v = snap[f]
                self.waited[engine][f] = v
                self.queue[engine].append(([(f, v)], False, None, 0))

    def finalize(self, final_engine="sync"):
        snap = self.barrier_value()
        self.wait_all(final_engine, snap)
        waited_vals = {e: set() for e in ENGINES}
        for e in ENGINES:
            for waits, drain, fn, inc in self.queue[e]:
                for (f, v) in waits:
                    waited_vals[f].add(v)
        block = self.stack.enter_context(self.nc.Block())
        drain_engines = ("vector", "scalar", "gpsimd")
        for e in ENGINES:
            q = self.queue[e]
            if not q and e != final_engine:
                continue
            sem = self.sems[e]

            def body(eng, q=q, sem=sem, e=e):
                cnt = 0
                for waits, drain, fn, inc in q:
                    for (f, v) in waits:
                        eng.wait_ge(self.sems[f], v)
                    if drain:
                        eng.drain()
                    if fn is not None:
                        inst = fn(eng)
                        cnt += inc
                        if inc == 1 and e in drain_engines and cnt in waited_vals[e]:
                            eng.drain().then_inc(sem, 1)
                        else:
                            inst.then_inc(sem, inc)
                if e == final_engine:
                    eng.wait_ge(sem, self.cnt[e])
            getattr(block, e)(body)



def dma(p, out, in_, reads, writes, engine="sync"):
    p.op(engine, lambda e: e.dma_start(out=out, in_=in_), reads=reads, writes=writes, dma=True)


def ones_col_np():
    return np.ones((128, 1), np.float32)


GRP = 512
NGRP = N // GRP
NW = 4
H = 16
DH = 64


def build_attn0(debug=False):
    nc = bass.Bass("TRN2", target_bir_lowering=False)
    xT = nc.dram_tensor("xT", [B_LOC, D, N], F32, kind="ExternalInput").ap()
    wqkv = nc.dram_tensor("wqkv", [D, 3 * D], F32, kind="ExternalInput").ap()
    wout = nc.dram_tensor("wout", [D, D], F32, kind="ExternalInput").ap()
    gcol = nc.dram_tensor("gcol", [128, 8], F32, kind="ExternalInput").ap()
    bcol = nc.dram_tensor("bcol", [128, 8], F32, kind="ExternalInput").ap()
    maskg = nc.dram_tensor("maskg", [128, 256], F32, kind="ExternalInput").ap()
    mask0 = nc.dram_tensor("mask0", [128, 256], F32, kind="ExternalInput").ap()
    identb = nc.dram_tensor("identb", [128, 128], BF16, kind="ExternalInput").ap()
    ident2 = nc.dram_tensor("ident2", [128, 64], BF16, kind="ExternalInput").ap()
    ones1 = nc.dram_tensor("ones1", [1, 128], F32, kind="ExternalInput").ap()
    onesc = nc.dram_tensor("onesc", [128, 1], F32, kind="ExternalInput").ap()
    y1T = nc.dram_tensor("y1T", [B_LOC, D, N], F32, kind="ExternalOutput").ap()
    if debug:
        dbg_qkv = nc.dram_tensor("dbg_qkv", [2, 3072, GRP], F32, kind="ExternalOutput").ap()
        dbg_xn = nc.dram_tensor("dbg_xn", [2, D, GRP], F32, kind="ExternalOutput").ap()
        dbg_z1 = nc.dram_tensor("dbg_z1", [D, GRP], F32, kind="ExternalOutput").ap()

    with ExitStack() as st:
        e = st.enter_context
        s_wqkv = e(nc.sbuf_tensor("s_wqkv", [128, 8 * 3072], BF16))
        s_wst = e(nc.sbuf_tensor("s_wst", [128, 3072], F32))
        s_wout = e(nc.sbuf_tensor("s_wout", [128, 8 * 1024], BF16))
        s_gcol = e(nc.sbuf_tensor("s_gcol", [128, 8], F32))
        s_bcol = e(nc.sbuf_tensor("s_bcol", [128, 8], F32))
        s_maskg = e(nc.sbuf_tensor("s_maskg", [128, 256], F32))
        s_mask0 = e(nc.sbuf_tensor("s_mask0", [128, 256], F32))
        s_id = e(nc.sbuf_tensor("s_id", [128, 128], BF16))
        s_id2 = e(nc.sbuf_tensor("s_id2", [128, 64], BF16))
        s_ones1 = e(nc.sbuf_tensor("s_ones1", [1, 128], F32))
        s_onesc = e(nc.sbuf_tensor("s_onesc", [128, 1], F32))
        s_x = [e(nc.sbuf_tensor("s_x0", [128, 8 * GRP], F32))] * 2
        s_z = e(nc.sbuf_tensor("s_z", [128, 8 * GRP], F32))
        s_zsq = e(nc.sbuf_tensor("s_zsq", [128, GRP], F32))
        s_xn = e(nc.sbuf_tensor("s_xn", [128, 8 * GRP], BF16))
        s_qkv = [e(nc.sbuf_tensor(f"s_qkv{i}", [128, 24 * GRP], BF16)) for i in range(2)]
        s_mu = e(nc.sbuf_tensor("s_mu", [1, GRP], F32))
        s_var = e(nc.sbuf_tensor("s_var", [1, GRP], F32))
        s_rstd = e(nc.sbuf_tensor("s_rstd", [1, GRP], F32))
        s_m2 = e(nc.sbuf_tensor("s_m2", [1, GRP], F32))
        s_simf = e(nc.sbuf_tensor("s_simf", [128, 256], F32))
        s_negmax = e(nc.sbuf_tensor("s_negmax", [128, 1], F32))
        s_ssum = e(nc.sbuf_tensor("s_ssum", [128, 1], F32))
        s_rsum = e(nc.sbuf_tensor("s_rsum", [128, 1], F32))
        s_attn = e(nc.sbuf_tensor("s_attn", [128, 256], BF16))
        s_attnT = e(nc.sbuf_tensor("s_attnT", [128, 256], BF16))
        s_vtok = [[e(nc.sbuf_tensor(f"s_vtok{h}_{i}", [128, DH], BF16)) for i in range(2)] for h in range(H)]
        s_aoutT = e(nc.sbuf_tensor("s_aoutT", [128, 8 * GRP], BF16))
        s_y1 = e(nc.sbuf_tensor("s_y1", [128, GRP], F32))
        p_stat = e(nc.psum_tensor("p_stat", [1, GRP], F32))
        p_bcast = e(nc.psum_tensor("p_bcast", [128, GRP], F32))
        p_qkv = e(nc.psum_tensor("p_qkv", [128, GRP], F32))
        p_sim = e(nc.psum_tensor("p_sim", [128, 256], F32))
        p_tr = e(nc.psum_tensor("p_tr", [128, 256], BF16))
        p_av = e(nc.psum_tensor("p_av", [64, 128], F32))
        p_out = e(nc.psum_tensor("p_out", [128, GRP], F32))

        p = Prog(nc, st)

        dma(p, s_gcol[:], gcol, [], [s_gcol])
        dma(p, s_bcol[:], bcol, [], [s_bcol])
        dma(p, s_maskg[:], maskg, [], [s_maskg])
        dma(p, s_mask0[:], mask0, [], [s_mask0])
        dma(p, s_id[:], identb, [], [s_id])
        dma(p, s_id2[:], ident2, [], [s_id2])
        dma(p, s_ones1[:], ones1, [], [s_ones1])
        dma(p, s_onesc[:], onesc, [], [s_onesc])
        for kt in range(8):
            dma(p, s_wst[:], wqkv[kt * 128:(kt + 1) * 128, :], [], [s_wst])
            p.op("vector", lambda e, kt=kt: e.tensor_copy(
                out=s_wqkv[:, kt * 3072:(kt + 1) * 3072], in_=s_wst[:]),
                reads=[s_wst], writes=[s_wqkv])
            dma(p, s_wst[:, 0:1024], wout[kt * 128:(kt + 1) * 128, :], [], [s_wst])
            p.op("vector", lambda e, kt=kt: e.tensor_copy(
                out=s_wout[:, kt * 1024:(kt + 1) * 1024], in_=s_wst[:, 0:1024]),
                reads=[s_wst], writes=[s_wout])

        def ln_stats(srcbuf, outmu_consumer):
            sa_ = srcbuf.ap()
            for kt in range(8):
                p.op("tensor", lambda e, kt=kt: e.matmul(
                    out=p_stat[:], lhsT=s_onesc[:], rhs=sa_[:, kt * GRP:(kt + 1) * GRP],
                    start=(kt == 0), stop=(kt == 7)), reads=[s_onesc, srcbuf], writes=[p_stat])
            p.op("scalar", lambda e: e.activation(out=s_mu[:], in_=p_stat[:], func=AF.Copy, scale=1.0 / D),
                 reads=[p_stat], writes=[s_mu], small=True)
            for kt in range(8):
                p.op("scalar", lambda e, kt=kt: e.activation(
                    out=s_zsq[:], in_=sa_[:, kt * GRP:(kt + 1) * GRP], func=AF.Square),
                    reads=[srcbuf], writes=[s_zsq])
                p.op("tensor", lambda e, kt=kt: e.matmul(
                    out=p_stat[:], lhsT=s_onesc[:], rhs=s_zsq[:],
                    start=(kt == 0), stop=(kt == 7)), reads=[s_onesc, s_zsq], writes=[p_stat])
            p.op("scalar", lambda e: e.activation(out=s_m2[:], in_=p_stat[:], func=AF.Copy, scale=1.0 / D),
                 reads=[p_stat], writes=[s_m2], small=True)
            p.op("vector", lambda e: e.tensor_mul(out=s_var[:], in0=s_mu[:], in1=s_mu[:]),
                 reads=[s_mu], writes=[s_var], small=True)
            p.op("vector", lambda e: e.tensor_sub(out=s_var[:], in0=s_m2[:], in1=s_var[:]),
                 reads=[s_m2, s_var], writes=[s_var], small=True)
            p.op("vector", lambda e: e.tensor_scalar_add(out=s_var[:], in0=s_var[:], scalar1=1e-5),
                 reads=[s_var], writes=[s_var], small=True)
            p.op("scalar", lambda e: e.activation(out=s_rstd[:], in_=s_var[:], func=AF.Ln),
                 reads=[s_var], writes=[s_rstd], small=True)
            p.op("scalar", lambda e: e.activation(out=s_rstd[:], in_=s_rstd[:], func=AF.Exp, scale=-0.5),
                 reads=[s_rstd], writes=[s_rstd], small=True)

        def ln_apply(srcbuf, dstbuf):
            sa_, da_ = srcbuf.ap(), dstbuf.ap()
            p.op("tensor", lambda e: e.matmul(out=p_bcast[:], lhsT=s_ones1[:], rhs=s_mu[:],
                                              start=True, stop=True), reads=[s_ones1, s_mu], writes=[p_bcast])
            for kt in range(8):
                p.op("vector", lambda e, kt=kt: e.tensor_sub(
                    out=da_[:, kt * GRP:(kt + 1) * GRP], in0=sa_[:, kt * GRP:(kt + 1) * GRP],
                    in1=p_bcast[:]), reads=[srcbuf, p_bcast], writes=[dstbuf])
            p.op("tensor", lambda e: e.matmul(out=p_bcast[:], lhsT=s_ones1[:], rhs=s_rstd[:],
                                              start=True, stop=True), reads=[s_ones1, s_rstd], writes=[p_bcast])
            for kt in range(8):
                p.op("vector", lambda e, kt=kt: e.tensor_mul(
                    out=da_[:, kt * GRP:(kt + 1) * GRP], in0=da_[:, kt * GRP:(kt + 1) * GRP],
                    in1=p_bcast[:]), reads=[dstbuf, p_bcast], writes=[dstbuf])

        def ln_plain_then_affine(xbuf, gsel):
            ln_stats(xbuf, None)
            ln_apply(xbuf, s_z)
            if debug and gsel == "dump":
                for kt in range(8):
                    dma(p, dbg_z1[kt * 128:(kt + 1) * 128, :], s_z.ap()[:, kt * GRP:(kt + 1) * GRP], [s_z], ["dbg2"])
            ln_stats(s_z, None)
            za = s_z.ap()
            xna = s_xn.ap()
            p.op("tensor", lambda e: e.matmul(out=p_bcast[:], lhsT=s_ones1[:], rhs=s_mu[:],
                                              start=True, stop=True), reads=[s_ones1, s_mu], writes=[p_bcast])
            for kt in range(8):
                p.op("vector", lambda e, kt=kt: e.tensor_sub(
                    out=za[:, kt * GRP:(kt + 1) * GRP], in0=za[:, kt * GRP:(kt + 1) * GRP],
                    in1=p_bcast[:]), reads=[s_z, p_bcast], writes=[s_z])
            p.op("tensor", lambda e: e.matmul(out=p_bcast[:], lhsT=s_ones1[:], rhs=s_rstd[:],
                                              start=True, stop=True), reads=[s_ones1, s_rstd], writes=[p_bcast])
            for kt in range(8):
                p.op("vector", lambda e, kt=kt: e.tensor_mul(
                    out=za[:, kt * GRP:(kt + 1) * GRP], in0=za[:, kt * GRP:(kt + 1) * GRP],
                    in1=p_bcast[:]), reads=[s_z, p_bcast], writes=[s_z])
            for kt in range(8):
                p.op("vector", lambda e, kt=kt: e.tensor_scalar(
                    out=xna[:, kt * GRP:(kt + 1) * GRP], in0=za[:, kt * GRP:(kt + 1) * GRP],
                    scalar1=s_gcol[:, kt:kt + 1], scalar2=s_bcol[:, kt:kt + 1],
                    op0=ALU.mult, op1=ALU.add), reads=[s_z, s_gcol, s_bcol], writes=[s_xn])

        for h in range(H):
            p.op("vector", lambda e, h=h: e.memset(s_vtok[h][1][:], 0.0), reads=[], writes=[s_vtok[h][1]])
        for b in range(B_LOC):
            for g in range(NGRP):
                xb = s_x[0]
                for kt in range(8):
                    dma(p, xb.ap()[:, kt * GRP:(kt + 1) * GRP],
                        xT[b, kt * 128:(kt + 1) * 128, g * GRP:(g + 1) * GRP], [], [xb])
                ln_plain_then_affine(xb, "dump" if (debug and b == 0 and g == 0) else g)
                p.barrier()
                cur = s_qkv[g % 2]
                prev = s_qkv[(g + 1) % 2]
                ca = cur.ap()
                pa = prev.ap()
                if g == 0:
                    p.op("vector", lambda e, pa=pa: e.memset(pa[:, 8 * GRP:24 * GRP], 0.0),
                         reads=[], writes=[prev])
                xna = s_xn.ap()
                for fi in range(24):
                    for kt in range(8):
                        p.op("tensor", lambda e, fi=fi, kt=kt, xna=xna: e.matmul(
                            out=p_qkv[:], lhsT=s_wqkv[:, kt * 3072 + fi * 128: kt * 3072 + (fi + 1) * 128],
                            rhs=xna[:, kt * GRP:(kt + 1) * GRP],
                            start=(kt == 0), stop=(kt == 7)), reads=[s_wqkv, s_xn], writes=[p_qkv])
                    p.op("scalar", lambda e, fi=fi, ca=ca: e.copy(
                        out=ca[:, fi * GRP:(fi + 1) * GRP], in_=p_qkv[:]),
                        reads=[p_qkv], writes=[cur])
                if debug and b == 0 and g < 2:
                    for fi in range(24):
                        p.op("vector", lambda e, fi=fi, ca=ca: e.tensor_copy(out=s_y1[:], in_=ca[:, fi * GRP:(fi + 1) * GRP]),
                             reads=[cur], writes=[s_y1])
                        dma(p, dbg_qkv[g, fi * 128:(fi + 1) * 128, :], s_y1[:], [s_y1], ["dbg"])
                    for kt in range(8):
                        p.op("vector", lambda e, kt=kt: e.tensor_copy(out=s_y1[:], in_=s_xn.ap()[:, kt * GRP:(kt + 1) * GRP]),
                             reads=[s_xn], writes=[s_y1])
                        dma(p, dbg_xn[g, kt * 128:(kt + 1) * 128, :], s_y1[:], [s_y1], ["dbg"])
                p.barrier()
                for wloc in range(NW):
                    w = g * NW + wloc
                    c0 = wloc * 128
                    for h in range(H):
                        qt_tile = (64 * h) // 128 + 0
                        qt_base = (64 * h) % 128
                        kt_tile = (1024 + 64 * h) // 128
                        kt_base = (1024 + 64 * h) % 128
                        vt_tile = (2048 + 64 * h) // 128
                        vt_base = (2048 + 64 * h) % 128
                        vring = s_vtok[h][w % 2]
                        p.op("tensor", lambda e, vt_tile=vt_tile, vt_base=vt_base, c0=c0, ca=ca: e.transpose(
                            out=p_tr[:, 0:64], in_=ca[vt_base:vt_base + 64, vt_tile * GRP + c0: vt_tile * GRP + c0 + 128],
                            identity=s_id2[vt_base:vt_base + 64, :]), reads=[cur, s_id2], writes=[p_tr])
                        p.op("scalar", lambda e, vring=vring: e.copy(out=vring[:], in_=p_tr[:, 0:64]),
                             reads=[p_tr], writes=[vring])
                        if wloc == 0:
                            kprev_buf, kprev_t, kprev_c = prev, kt_tile, 384
                            kprev_key = prev
                        else:
                            kprev_buf, kprev_t, kprev_c = cur, kt_tile, c0 - 128
                            kprev_key = cur
                        p.op("tensor", lambda e, kprev_buf=kprev_buf, kprev_t=kprev_t, kprev_c=kprev_c,
                             kt_base=kt_base, qt_tile=qt_tile, qt_base=qt_base, c0=c0, ca=ca: e.matmul(
                            out=p_sim[:, 0:128],
                            lhsT=ca[qt_base:qt_base + 64, qt_tile * GRP + c0: qt_tile * GRP + c0 + 128],
                            rhs=kprev_buf.ap()[kt_base:kt_base + 64, kprev_t * GRP + kprev_c: kprev_t * GRP + kprev_c + 128],
                            start=True, stop=True), reads=[cur, kprev_key], writes=[p_sim])
                        p.op("tensor", lambda e, kt_tile=kt_tile, kt_base=kt_base, qt_tile=qt_tile,
                             qt_base=qt_base, c0=c0, ca=ca: e.matmul(
                            out=p_sim[:, 128:256],
                            lhsT=ca[qt_base:qt_base + 64, qt_tile * GRP + c0: qt_tile * GRP + c0 + 128],
                            rhs=ca[kt_base:kt_base + 64, kt_tile * GRP + c0: kt_tile * GRP + c0 + 128],
                            start=True, stop=True), reads=[cur], writes=[p_sim])
                        mask = s_mask0 if w == 0 else s_maskg
                        p.op("vector", lambda e, mask=mask: e.scalar_tensor_tensor(
                            out=s_simf[:], in0=p_sim[:], scalar=DH ** -0.5, in1=mask[:],
                            op0=ALU.mult, op1=ALU.add), reads=[p_sim, mask], writes=[s_simf])
                        p.op("vector", lambda e: e.reduce_max(out=s_negmax[:], in_=s_simf[:], axis=AX, negate=True),
                             reads=[s_simf], writes=[s_negmax], small=True)
                        p.op("scalar", lambda e: e.activation(out=s_simf[:], in_=s_simf[:], func=AF.Exp,
                                                              bias=s_negmax[:, 0:1], accum_out=s_ssum[:]),
                             reads=[s_simf, s_negmax], writes=[s_simf, s_ssum], small=True)
                        p.op("vector", lambda e: e.reciprocal(out=s_rsum[:], in_=s_ssum[:]),
                             reads=[s_ssum], writes=[s_rsum], small=True)
                        p.op("vector", lambda e: e.tensor_scalar(out=s_attn[:], in0=s_simf[:],
                                                                 scalar1=s_rsum[:, 0:1], scalar2=None, op0=ALU.mult),
                             reads=[s_simf, s_rsum], writes=[s_attn])
                        for half in range(2):
                            p.op("tensor", lambda e, half=half: e.transpose(
                                out=p_tr[:, 128:256], in_=s_attn[:, half * 128:(half + 1) * 128],
                                identity=s_id[:]), reads=[s_attn, s_id], writes=[p_tr])
                            p.op("scalar", lambda e, half=half: e.copy(
                                out=s_attnT[:, half * 128:(half + 1) * 128], in_=p_tr[:, 128:256]),
                                reads=[p_tr], writes=[s_attnT])
                        vprev = s_vtok[h][(w + 1) % 2]
                        p.op("tensor", lambda e, vprev=vprev: e.matmul(
                            out=p_av[:], lhsT=vprev[:], rhs=s_attnT[:, 0:128],
                            start=True, stop=False), reads=[vprev, s_attnT], writes=[p_av])
                        p.op("tensor", lambda e, vring=vring: e.matmul(
                            out=p_av[:], lhsT=vring[:], rhs=s_attnT[:, 128:256],
                            start=False, stop=True), reads=[vring, s_attnT], writes=[p_av])
                        ao_tile = (64 * h) // 128
                        ao_base = (64 * h) % 128
                        p.op("scalar", lambda e, ao_tile=ao_tile, ao_base=ao_base, c0=c0: e.copy(
                            out=s_aoutT.ap()[ao_base:ao_base + 64, ao_tile * GRP + c0: ao_tile * GRP + c0 + 128],
                            in_=p_av[:]), reads=[p_av], writes=[s_aoutT])
                p.barrier()
                aoa = s_aoutT.ap()
                for j in range(8):
                    for kt in range(8):
                        p.op("tensor", lambda e, j=j, kt=kt, aoa=aoa: e.matmul(
                            out=p_out[:], lhsT=s_wout[:, kt * 1024 + j * 128: kt * 1024 + (j + 1) * 128],
                            rhs=aoa[:, kt * GRP:(kt + 1) * GRP],
                            start=(kt == 0), stop=(kt == 7)), reads=[s_wout, s_aoutT], writes=[p_out])
                    p.op("vector", lambda e, j=j, xb=xb: e.tensor_add(
                        out=s_y1[:], in0=xb.ap()[:, j * GRP:(j + 1) * GRP], in1=p_out[:]),
                        reads=[xb, p_out], writes=[s_y1])
                    dma(p, y1T[b, j * 128:(j + 1) * 128, g * GRP:(g + 1) * GRP], s_y1[:],
                        [s_y1], ["y1T_out"])
        p.finalize()
    return nc



def _attn_masks():
    i = np.arange(128)[:, None]
    j = np.arange(256)[None, :]
    allowed_g = (j >= i) & (j <= i + 128)
    allowed_0 = allowed_g & (j >= 128)
    mg = np.where(allowed_g, 0.0, NEG).astype(np.float32)
    m0 = np.where(allowed_0, 0.0, NEG).astype(np.float32)
    return mg, m0


def kernel(**inp):
    import ml_dtypes
    x = np.asarray(inp["L0_attn_x"], np.float32)
    xT = np.ascontiguousarray(x.transpose(0, 2, 1))

    mg, m0 = _attn_masks()
    identb = np.eye(128, dtype=np.float32).astype(ml_dtypes.bfloat16)
    ident2 = np.ascontiguousarray(np.vstack([np.eye(64), np.eye(64)]).astype(ml_dtypes.bfloat16))
    identf = np.eye(128, dtype=np.float32)
    ones1 = np.ones((1, 128), np.float32)
    onesc = np.ones((128, 1), np.float32)
    cores = list(range(NCORES))

    def run(nc, maps):
        return run_bass_kernel_spmd(nc, maps, core_ids=cores)

    def gathT(res, key):
        return np.stack([np.array(r[key]) for r in res.results])

    gcol = np.ascontiguousarray(np.asarray(inp["L0_attn_ln_gamma"], np.float32).reshape(8, 128).T)
    bcol = np.ascontiguousarray(np.asarray(inp["L0_attn_ln_beta"], np.float32).reshape(8, 128).T)
    nc1 = build_attn0()
    maps = [dict(xT=np.ascontiguousarray(xT[2 * c:2 * c + 2]),
                 wqkv=np.ascontiguousarray(np.asarray(inp["L0_attn_w_qkv"], np.float32)),
                 wout=np.ascontiguousarray(np.asarray(inp["L0_attn_w_out"], np.float32)),
                 gcol=gcol, bcol=bcol, maskg=mg, mask0=m0,
                 identb=identb, ident2=ident2, ones1=ones1, onesc=onesc) for c in range(NCORES)]
    y1T_sh = gathT(run(nc1, maps), "y1T")

    nc2 = build_scores()
    maps = [dict(y1T=np.ascontiguousarray(y1T_sh[c]),
                 Wr=np.ascontiguousarray(np.asarray(inp["L0_ffn_Wr"], np.float32)),
                 brrow=np.ascontiguousarray(np.asarray(inp["L0_ffn_br"], np.float32)[None, :]),
                 ones1=ones1, onesc=onesc) for c in range(NCORES)]
    res2 = run(nc2, maps)
    scores = np.concatenate([np.array(r["scores"]) for r in res2.results])
    zTb_sh = np.stack([np.array(r["zTb"]) for r in res2.results])

    scT = np.ascontiguousarray(scores.T)
    nc3 = build_routing()
    maps = [dict(scT=scT, scloc=np.ascontiguousarray(scores[4096 * c:4096 * (c + 1)]),
                 ones1=ones1, onesc=onesc, identf=identf) for c in range(NCORES)]
    res3 = run(nc3, maps)
    fs_sh = [np.array(r["fsloc"]) for r in res3.results]

    bupT2 = np.ascontiguousarray(np.asarray(inp["L0_ffn_bup"], np.float32).T.reshape(2, 128, 64))
    bgateT2 = np.ascontiguousarray(np.asarray(inp["L0_ffn_bgate"], np.float32).T.reshape(2, 128, 64))
    nc4a = build_ffn0_h()
    maps = [dict(zTb=np.ascontiguousarray(zTb_sh[c]),
                 fsT=np.ascontiguousarray(fs_sh[c].T),
                 Wup=np.ascontiguousarray(np.asarray(inp["L0_ffn_Wup"], np.float32)),
                 Wgate=np.ascontiguousarray(np.asarray(inp["L0_ffn_Wgate"], np.float32)),
                 bupT2=bupT2, bgateT2=bgateT2, ones1=ones1) for c in range(NCORES)]
    res4a = run(nc4a, maps)
    h_sh = [np.array(r["hbuf"]) for r in res4a.results]

    wdncat = np.ascontiguousarray(np.asarray(inp["L0_ffn_Wdown"], np.float32).reshape(16384, 1024))
    nc4b = build_ffn0_down()
    maps = [dict(hbuf=h_sh[c], wdncat=wdncat,
                 bdown=np.ascontiguousarray(np.asarray(inp["L0_ffn_bdown"], np.float32)),
                 fsT=np.ascontiguousarray(fs_sh[c].T),
                 y1T=np.ascontiguousarray(y1T_sh[c])) for c in range(NCORES)]
    y2T_sh = gathT(run(nc4b, maps), "y2T")

    cw = np.asarray(inp["L1_attn_conv_w"], np.float32)[:, 0, :]
    cwcol = np.ascontiguousarray(cw.reshape(16, 128, 4).transpose(1, 0, 2))
    cbcol = np.ascontiguousarray(np.asarray(inp["L1_attn_conv_b"], np.float32).reshape(16, 128).T)
    bdtcol = np.ascontiguousarray(np.asarray(inp["L1_attn_b_dt"], np.float32).reshape(16, 128).T)
    nc5 = build_ssm_pre()
    maps = [dict(y2T=np.ascontiguousarray(y2T_sh[c]),
                 WinT=np.ascontiguousarray(np.asarray(inp["L1_attn_W_in"], np.float32).T),
                 WdtT=np.ascontiguousarray(np.asarray(inp["L1_attn_W_dt"], np.float32).T),
                 WBT=np.ascontiguousarray(np.asarray(inp["L1_attn_W_B"], np.float32).T),
                 WCT=np.ascontiguousarray(np.asarray(inp["L1_attn_W_C"], np.float32).T),
                 cwcol=cwcol, cbcol=cbcol, bdtcol=bdtcol,
                 ones1=ones1, onesc=onesc) for c in range(NCORES)]
    res5 = run(nc5, maps)
    r5 = [{k: np.array(v) for k, v in r.items()} for r in res5.results]

    Alog = np.asarray(inp["L1_attn_A_log"], np.float32)
    Acols = np.ascontiguousarray(Alog.reshape(256, 8, 16).transpose(1, 2, 0).reshape(128, 256))
    sel = np.zeros((128, 16, 128), np.float32)
    for jj in range(16):
        for pp in range(128):
            sel[8 * jj + pp // 16, jj, pp] = 1.0
    repB = np.zeros((16, 128), np.float32)
    for pp in range(128):
        repB[pp % 16, pp] = 1.0
    red8 = np.zeros((128, 8), np.float32)
    for kk in range(128):
        red8[kk, kk // 16] = 1.0
    nc6 = build_scan()
    maps = [dict(xvT=r5[c]["xvT"], dtT=r5[c]["dtT"], BT=r5[c]["BT"], CT=r5[c]["CT"],
                 Acols=Acols, sel=np.ascontiguousarray(sel), repB=repB, red8=red8)
            for c in range(NCORES)]
    res6 = run(nc6, maps)
    yT_sh = [np.array(r["yT"]) for r in res6.results]

    nc7 = build_ssm_post()
    maps = [dict(yT=yT_sh[c], zsT=r5[c]["zsT"], y2T=np.ascontiguousarray(y2T_sh[c]),
                 WoutT=np.ascontiguousarray(np.asarray(inp["L1_attn_W_out"], np.float32).T))
            for c in range(NCORES)]
    y3T_sh = gathT(run(nc7, maps), "y3T")

    sp1 = np.asarray(inp["L1_ffn_spline_w1"], np.float32).reshape(64, 32)
    sb1 = np.asarray(inp["L1_ffn_spline_b1"], np.float32)
    sp2 = np.asarray(inp["L1_ffn_spline_w2"], np.float32).reshape(64, 16)
    sb2 = np.asarray(inp["L1_ffn_spline_b2"], np.float32)
    spcat = np.ascontiguousarray(np.concatenate([sp1, sb1, sp2, sb2], axis=1))
    nc8 = build_ffn1()
    maps = [dict(
        y3T=np.ascontiguousarray(y3T_sh[c]),
        sigsrawT=np.ascontiguousarray(np.asarray(inp["L1_ffn_signatures_raw"], np.float32).T),
        knots=np.ascontiguousarray(np.asarray(inp["L1_ffn_knot_values"], np.float32)[None, :]),
        temp=np.ascontiguousarray(np.asarray(inp["L1_ffn_temperature"], np.float32)[None, :]),
        cw1=np.ascontiguousarray(np.asarray(inp["L1_ffn_compress_w1"], np.float32)),
        cb1col=np.ascontiguousarray(np.asarray(inp["L1_ffn_compress_b1"], np.float32).reshape(2, 128).T),
        cw2=np.ascontiguousarray(np.asarray(inp["L1_ffn_compress_w2"], np.float32)),
        cb2col=np.ascontiguousarray(np.asarray(inp["L1_ffn_compress_b2"], np.float32)[:, None]),
        spcat=spcat,
        dirs=np.ascontiguousarray(np.asarray(inp["L1_ffn_directions"], np.float32)),
        gcol=np.ascontiguousarray(np.asarray(inp["L1_ffn_norm_gamma"], np.float32).reshape(8, 128).T),
        bcol=np.ascontiguousarray(np.asarray(inp["L1_ffn_norm_beta"], np.float32).reshape(8, 128).T),
        osc=np.ascontiguousarray(np.asarray(inp["L1_ffn_output_scale"], np.float32)[None, :]),
        identf=identf, ones1=ones1, onesc=onesc) for c in range(NCORES)]
    res8 = run(nc8, maps)
    outT = np.stack([np.array(r["outT"]) for r in res8.results])
    out = outT.reshape(16, 1024, 2048).transpose(0, 2, 1)
    return np.ascontiguousarray(out)



def build_scores():
    nc = bass.Bass("TRN2", target_bir_lowering=False)
    y1T = nc.dram_tensor("y1T", [B_LOC, D, N], F32, kind="ExternalInput").ap()
    Wr = nc.dram_tensor("Wr", [D, 64], F32, kind="ExternalInput").ap()
    brrow = nc.dram_tensor("brrow", [1, 64], F32, kind="ExternalInput").ap()
    ones1 = nc.dram_tensor("ones1", [1, 128], F32, kind="ExternalInput").ap()
    onesc = nc.dram_tensor("onesc", [128, 1], F32, kind="ExternalInput").ap()
    scores = nc.dram_tensor("scores", [B_LOC * N, 64], F32, kind="ExternalOutput").ap()
    zTb = nc.dram_tensor("zTb", [B_LOC, D, N], BF16, kind="ExternalOutput").ap()

    with ExitStack() as st:
        e = st.enter_context
        s_y = e(nc.sbuf_tensor("s_y", [128, 8 * GRP], F32))
        s_z = e(nc.sbuf_tensor("s_z", [128, 8 * GRP], F32))
        s_zb = e(nc.sbuf_tensor("s_zb", [128, 8 * GRP], BF16))
        s_zsq = e(nc.sbuf_tensor("s_zsq", [128, GRP], F32))
        s_mu = e(nc.sbuf_tensor("s_mu", [1, GRP], F32))
        s_m2 = e(nc.sbuf_tensor("s_m2", [1, GRP], F32))
        s_var = e(nc.sbuf_tensor("s_var", [1, GRP], F32))
        s_rstd = e(nc.sbuf_tensor("s_rstd", [1, GRP], F32))
        s_Wr = e(nc.sbuf_tensor("s_Wr", [128, 8 * 64], F32))
        s_brb = e(nc.sbuf_tensor("s_brb", [128, 64], F32))
        s_brrow = e(nc.sbuf_tensor("s_brrow", [1, 64], F32))
        s_ones1 = e(nc.sbuf_tensor("s_ones1", [1, 128], F32))
        s_onesc = e(nc.sbuf_tensor("s_onesc", [128, 1], F32))
        s_l = e(nc.sbuf_tensor("s_l", [128, 64], F32))
        s_nm = e(nc.sbuf_tensor("s_nm", [128, 1], F32))
        s_sum = e(nc.sbuf_tensor("s_sum", [128, 1], F32))
        s_rs = e(nc.sbuf_tensor("s_rs", [128, 1], F32))
        s_sc = e(nc.sbuf_tensor("s_sc", [128, 64], F32))
        p_stat = e(nc.psum_tensor("p_stat", [1, GRP], F32))
        p_bcast = e(nc.psum_tensor("p_bcast", [128, GRP], F32))
        p_sc = e(nc.psum_tensor("p_sc", [128, 64], F32))
        p_br = e(nc.psum_tensor("p_br", [128, 64], F32))
        p = Prog(nc, st)

        dma(p, s_ones1[:], ones1, [], [s_ones1])
        dma(p, s_onesc[:], onesc, [], [s_onesc])
        dma(p, s_brrow[:], brrow, [], [s_brrow])
        for kt in range(8):
            dma(p, s_Wr[:, kt * 64:(kt + 1) * 64], Wr[kt * 128:(kt + 1) * 128, :], [], [s_Wr])
        p.op("tensor", lambda e: e.matmul(out=p_br[:], lhsT=s_ones1[:], rhs=s_brrow[:],
                                          start=True, stop=True), reads=[s_ones1, s_brrow], writes=[p_br])
        p.op("scalar", lambda e: e.copy(out=s_brb[:], in_=p_br[:]), reads=[p_br], writes=[s_brb])

        def ln_stats(srcbuf):
            sa_ = srcbuf.ap()
            for kt in range(8):
                p.op("tensor", lambda e, kt=kt, sa_=sa_: e.matmul(
                    out=p_stat[:], lhsT=s_onesc[:], rhs=sa_[:, kt * GRP:(kt + 1) * GRP],
                    start=(kt == 0), stop=(kt == 7)), reads=[s_onesc, srcbuf], writes=[p_stat])
            p.op("scalar", lambda e: e.activation(out=s_mu[:], in_=p_stat[:], func=AF.Copy, scale=1.0 / D),
                 reads=[p_stat], writes=[s_mu], small=True)
            for kt in range(8):
                p.op("scalar", lambda e, kt=kt, sa_=sa_: e.activation(
                    out=s_zsq[:], in_=sa_[:, kt * GRP:(kt + 1) * GRP], func=AF.Square),
                    reads=[srcbuf], writes=[s_zsq])
                p.op("tensor", lambda e, kt=kt: e.matmul(
                    out=p_stat[:], lhsT=s_onesc[:], rhs=s_zsq[:],
                    start=(kt == 0), stop=(kt == 7)), reads=[s_onesc, s_zsq], writes=[p_stat])
            p.op("scalar", lambda e: e.activation(out=s_m2[:], in_=p_stat[:], func=AF.Copy, scale=1.0 / D),
                 reads=[p_stat], writes=[s_m2], small=True)
            p.op("vector", lambda e: e.tensor_mul(out=s_var[:], in0=s_mu[:], in1=s_mu[:]),
                 reads=[s_mu], writes=[s_var], small=True)
            p.op("vector", lambda e: e.tensor_sub(out=s_var[:], in0=s_m2[:], in1=s_var[:]),
                 reads=[s_m2, s_var], writes=[s_var], small=True)
            p.op("vector", lambda e: e.tensor_scalar_add(out=s_var[:], in0=s_var[:], scalar1=1e-5),
                 reads=[s_var], writes=[s_var], small=True)
            p.op("scalar", lambda e: e.activation(out=s_rstd[:], in_=s_var[:], func=AF.Ln),
                 reads=[s_var], writes=[s_rstd], small=True)
            p.op("scalar", lambda e: e.activation(out=s_rstd[:], in_=s_rstd[:], func=AF.Exp, scale=-0.5),
                 reads=[s_rstd], writes=[s_rstd], small=True)

        def ln_apply(srcbuf, dstbuf):
            sa_, da_ = srcbuf.ap(), dstbuf.ap()
            p.op("tensor", lambda e: e.matmul(out=p_bcast[:], lhsT=s_ones1[:], rhs=s_mu[:],
                                              start=True, stop=True), reads=[s_ones1, s_mu], writes=[p_bcast])
            for kt in range(8):
                p.op("vector", lambda e, kt=kt, sa_=sa_, da_=da_: e.tensor_sub(
                    out=da_[:, kt * GRP:(kt + 1) * GRP], in0=sa_[:, kt * GRP:(kt + 1) * GRP],
                    in1=p_bcast[:]), reads=[srcbuf, p_bcast], writes=[dstbuf])
            p.op("tensor", lambda e: e.matmul(out=p_bcast[:], lhsT=s_ones1[:], rhs=s_rstd[:],
                                              start=True, stop=True), reads=[s_ones1, s_rstd], writes=[p_bcast])
            for kt in range(8):
                p.op("vector", lambda e, kt=kt, da_=da_: e.tensor_mul(
                    out=da_[:, kt * GRP:(kt + 1) * GRP], in0=da_[:, kt * GRP:(kt + 1) * GRP],
                    in1=p_bcast[:]), reads=[dstbuf, p_bcast], writes=[dstbuf])

        for b in range(B_LOC):
            for g in range(NGRP):
                for kt in range(8):
                    dma(p, s_y.ap()[:, kt * GRP:(kt + 1) * GRP],
                        y1T[b, kt * 128:(kt + 1) * 128, g * GRP:(g + 1) * GRP], [], [s_y])
                ln_stats(s_y)
                ln_apply(s_y, s_z)
                for kt in range(8):
                    p.op("vector", lambda e, kt=kt: e.tensor_copy(
                        out=s_zb.ap()[:, kt * GRP:(kt + 1) * GRP], in_=s_z.ap()[:, kt * GRP:(kt + 1) * GRP]),
                        reads=[s_z], writes=[s_zb])
                    dma(p, zTb[b, kt * 128:(kt + 1) * 128, g * GRP:(g + 1) * GRP],
                        s_zb.ap()[:, kt * GRP:(kt + 1) * GRP], [s_zb], ["zout"])
                for tt in range(4):
                    for kt in range(8):
                        p.op("tensor", lambda e, kt=kt, tt=tt: e.matmul(
                            out=p_sc[:], lhsT=s_z.ap()[:, kt * GRP + tt * 128: kt * GRP + tt * 128 + 128],
                            rhs=s_Wr[:, kt * 64:(kt + 1) * 64],
                            start=(kt == 0), stop=(kt == 7)), reads=[s_z, s_Wr], writes=[p_sc])
                    p.op("vector", lambda e: e.tensor_add(out=s_l[:], in0=p_sc[:], in1=s_brb[:]),
                         reads=[p_sc, s_brb], writes=[s_l])
                    p.op("vector", lambda e: e.reduce_max(out=s_nm[:], in_=s_l[:], axis=AX, negate=True),
                         reads=[s_l], writes=[s_nm], small=True)
                    p.op("scalar", lambda e: e.activation(out=s_l[:], in_=s_l[:], func=AF.Exp,
                                                          bias=s_nm[:, 0:1], accum_out=s_sum[:]),
                         reads=[s_l, s_nm], writes=[s_l, s_sum], small=True)
                    p.op("vector", lambda e: e.reciprocal(out=s_rs[:], in_=s_sum[:]),
                         reads=[s_sum], writes=[s_rs], small=True)
                    p.op("vector", lambda e: e.tensor_scalar(out=s_sc[:], in0=s_l[:], scalar1=s_rs[:, 0:1],
                                                             scalar2=None, op0=ALU.mult),
                         reads=[s_l, s_rs], writes=[s_sc])
                    t0 = b * N + g * GRP + tt * 128
                    dma(p, scores[t0:t0 + 128, :], s_sc[:], [s_sc], ["scout"])
        p.finalize()
    return nc


def build_routing():
    nc = bass.Bass("TRN2", target_bir_lowering=False)
    scT = nc.dram_tensor("scT", [64, 32768], F32, kind="ExternalInput").ap()
    scloc = nc.dram_tensor("scloc", [4096, 64], F32, kind="ExternalInput").ap()
    ones1 = nc.dram_tensor("ones1", [1, 128], F32, kind="ExternalInput").ap()
    onesc = nc.dram_tensor("onesc", [128, 1], F32, kind="ExternalInput").ap()
    identf = nc.dram_tensor("identf", [128, 128], F32, kind="ExternalInput").ap()
    fsloc = nc.dram_tensor("fsloc", [4096, 64], F32, kind="ExternalOutput").ap()
    HN = 16384

    with ExitStack() as st:
        e = st.enter_context
        S = e(nc.sbuf_tensor("S", [128, HN], F32))
        s_tmp = e(nc.sbuf_tensor("s_tmp", [128, HN // 2], F32))
        s_bh = e(nc.sbuf_tensor("s_bh", [64, HN], F32))
        s_gsT = e(nc.sbuf_tensor("s_gsT", [64, 512], F32))
        s_bvT = e(nc.sbuf_tensor("s_bvT", [64, 512], F32))
        s_gs = e(nc.sbuf_tensor("s_gs", [128, 256], F32))
        s_bv = e(nc.sbuf_tensor("s_bv", [128, 256], F32))
        s_lo = e(nc.sbuf_tensor("s_lo", [64, 1], F32))
        s_hi = e(nc.sbuf_tensor("s_hi", [64, 1], F32))
        s_mid = e(nc.sbuf_tensor("s_mid", [64, 1], F32))
        s_cmp = e(nc.sbuf_tensor("s_cmp", [64, 512], F32))
        s_cnt = e(nc.sbuf_tensor("s_cnt", [64, 1], F32))
        s_c = e(nc.sbuf_tensor("s_c", [1, 1], F32))
        s_cu = e(nc.sbuf_tensor("s_cu", [64, 1], mybir.dt.uint8))
        s_big = e(nc.sbuf_tensor("s_big", [64, 512], F32))
        s_selm = e(nc.sbuf_tensor("s_selm", [64, 512], mybir.dt.uint8))
        s_vsel = e(nc.sbuf_tensor("s_vsel", [64, 512], F32))
        s_ve = e(nc.sbuf_tensor("s_ve", [64, 1], F32))
        s_vrow = e(nc.sbuf_tensor("s_vrow", [1, 64], F32))
        s_vb = e(nc.sbuf_tensor("s_vb", [128, 64], F32))
        s_idf = e(nc.sbuf_tensor("s_idf", [128, 128], F32))
        s_ones1 = e(nc.sbuf_tensor("s_ones1", [1, 128], F32))
        s_onesc = e(nc.sbuf_tensor("s_onesc", [128, 1], F32))
        s_sc = e(nc.sbuf_tensor("s_sc", [128, 64], F32))
        s_msk = e(nc.sbuf_tensor("s_msk", [128, 64], F32))
        s_fs = e(nc.sbuf_tensor("s_fs", [128, 64], F32))
        s_rsum = e(nc.sbuf_tensor("s_rsum", [128, 1], F32))
        p_cnt = e(nc.psum_tensor("p_cnt", [1, 1], F32))
        p_cb = e(nc.psum_tensor("p_cb", [64, 1], F32))
        p_vt = e(nc.psum_tensor("p_vt", [1, 64], F32))
        p_vb = e(nc.psum_tensor("p_vb", [128, 64], F32))
        p = Prog(nc, st)

        dma(p, s_ones1[:], ones1, [], [s_ones1])
        dma(p, s_onesc[:], onesc, [], [s_onesc])
        dma(p, s_idf[:], identf, [], [s_idf])
        dma(p, S.ap()[0:64, :], scT[:, 0:HN], [], [S])
        dma(p, S.ap()[64:128, :], scT[:, HN:2 * HN], [], [S])

        def cmpex(Av, Bv, tmpv, nel):
            p.op("vector", lambda e, Av=Av, Bv=Bv, tmpv=tmpv: e.tensor_tensor(
                out=tmpv, in0=Av, in1=Bv, op=ALU.min), reads=[S], writes=[s_tmp])
            p.op("vector", lambda e, Av=Av, Bv=Bv: e.tensor_tensor(
                out=Av, in0=Av, in1=Bv, op=ALU.max), reads=[S], writes=[S])
            p.op("vector", lambda e, Bv=Bv, tmpv=tmpv: e.tensor_copy(out=Bv, in_=tmpv),
                 reads=[s_tmp], writes=[S])

        Sap = S.ap()
        Tap = s_tmp.ap()
        L = 1
        while L < 32768:
            if 2 * L <= HN:
                va = Sap.rearrange("p (g two l) -> p g two l", two=2, l=L)
                A = va[:, :, 0, :]
                B = va[:, :, 1, ::-1]
                tm = Tap.rearrange("p (g l) -> p g l", l=L)
                cmpex(A, B, tm, 64 * HN // 2)
            else:
                dma(p, s_bh.ap()[:], Sap[64:128, :], [S], [s_bh])
                bhrev = s_bh.ap()[:, ::-1]
                for chunk in range(2):
                    c0, c1 = chunk * (HN // 2), (chunk + 1) * (HN // 2)
                    Ax = Sap[0:64, c0:c1]
                    Bx = bhrev[:, c0:c1]
                    Tx = Tap[0:64, :]
                    p.op("vector", lambda e, Ax=Ax, Bx=Bx, Tx=Tx: e.tensor_tensor(
                        out=Tx, in0=Ax, in1=Bx, op=ALU.min), reads=[S, s_bh], writes=[s_tmp])
                    p.op("vector", lambda e, Ax=Ax, Bx=Bx: e.tensor_tensor(
                        out=Ax, in0=Ax, in1=Bx, op=ALU.max), reads=[S, s_bh], writes=[S])
                    p.op("vector", lambda e, Bx=Bx, Tx=Tx: e.tensor_copy(out=Bx, in_=Tx),
                         reads=[s_tmp], writes=[s_bh])
                dma(p, Sap[64:128, :], s_bh.ap()[:], [s_bh], [S])
            run = min(2 * L, HN)
            d = L // 2 if 2 * L <= HN else HN // 2
            while d >= 1:
                va = Sap.rearrange("p (g two dd) -> p g two dd", two=2, dd=d)
                A = va[:, :, 0, :]
                B = va[:, :, 1, :]
                tm = Tap.rearrange("p (g dd) -> p g dd", dd=d)
                cmpex(A, B, tm, 64 * HN // 2)
                d //= 2
            L *= 2

        Sv = Sap.rearrange("p (g m) -> p g m", m=64)
        p.op("vector", lambda e: e.tensor_reduce(out=s_gs[:], in_=Sv, axis=AX, op=ALU.add),
             reads=[S], writes=[s_gs])
        p.op("vector", lambda e: e.tensor_copy(out=s_bv[:], in_=Sv[:, :, 63:64]),
             reads=[S], writes=[s_bv])
        dma(p, s_gsT.ap()[:, 0:256], s_gs.ap()[0:64, :], [s_gs], [s_gsT])
        dma(p, s_gsT.ap()[:, 256:512], s_gs.ap()[64:128, :], [s_gs], [s_gsT])
        dma(p, s_bvT.ap()[:, 0:256], s_bv.ap()[0:64, :], [s_bv], [s_bvT])
        dma(p, s_bvT.ap()[:, 256:512], s_bv.ap()[64:128, :], [s_bv], [s_bvT])

        p.op("vector", lambda e: e.memset(s_lo[:], 0.0), reads=[], writes=[s_lo], small=True)
        p.op("vector", lambda e: e.memset(s_hi[:], 64.0), reads=[], writes=[s_hi], small=True)
        p.op("vector", lambda e: e.memset(s_big[:], 3.0e38), reads=[], writes=[s_big])
        for it in range(48):
            p.op("vector", lambda e: e.tensor_scalar_mul(out=s_mid[:], in0=s_lo[:], scalar1=0.5),
                 reads=[s_lo], writes=[s_mid], small=True)
            p.op("vector", lambda e: e.scalar_tensor_tensor(out=s_mid[:], in0=s_hi[:], scalar=0.5,
                                                            in1=s_mid[:], op0=ALU.mult, op1=ALU.add),
                 reads=[s_hi, s_mid], writes=[s_mid], small=True)
            p.op("vector", lambda e: e.tensor_scalar(out=s_cmp[:], in0=s_gsT[:], scalar1=s_mid[:, 0:1],
                                                     scalar2=None, op0=ALU.is_ge),
                 reads=[s_gsT, s_mid], writes=[s_cmp])
            p.op("vector", lambda e: e.reduce_sum(out=s_cnt[:], in_=s_cmp[:], axis=AX),
                 reads=[s_cmp], writes=[s_cnt], small=True)
            p.op("tensor", lambda e: e.matmul(out=p_cnt[:], lhsT=s_onesc[0:64, :], rhs=s_cnt[:],
                                              start=True, stop=True), reads=[s_onesc, s_cnt], writes=[p_cnt])
            p.op("scalar", lambda e: e.copy(out=s_c[:], in_=p_cnt[:]), reads=[p_cnt], writes=[s_c], small=True)
            p.op("vector", lambda e: e.tensor_single_scalar(out=s_c[:], in_=s_c[:], scalar=8192.0, op=ALU.is_ge),
                 reads=[s_c], writes=[s_c], small=True)
            p.op("tensor", lambda e: e.matmul(out=p_cb[:], lhsT=s_ones1[0:1, 0:64], rhs=s_c[:],
                                              start=True, stop=True), reads=[s_ones1, s_c], writes=[p_cb])
            p.op("scalar", lambda e: e.copy(out=s_cnt[:], in_=p_cb[:]), reads=[p_cb], writes=[s_cnt], small=True)
            p.op("vector", lambda e: e.tensor_copy(out=s_cu[:], in_=s_cnt[:]),
                 reads=[s_cnt], writes=[s_cu], small=True)
            p.op("vector", lambda e: e.select(out=s_lo[:], mask=s_cu[:], on_true=s_mid[:], on_false=s_lo[:]),
                 reads=[s_cu, s_mid, s_lo], writes=[s_lo], small=True)
            p.op("vector", lambda e: e.select(out=s_hi[:], mask=s_cu[:], on_true=s_hi[:], on_false=s_mid[:]),
                 reads=[s_cu, s_mid, s_hi], writes=[s_hi], small=True)
        p.op("vector", lambda e: e.tensor_scalar(out=s_selm[:], in0=s_gsT[:], scalar1=s_lo[:, 0:1],
                                                 scalar2=None, op0=ALU.is_ge),
             reads=[s_gsT, s_lo], writes=[s_selm])
        p.op("vector", lambda e: e.select(out=s_vsel[:], mask=s_selm[:], on_true=s_bvT[:], on_false=s_big[:]),
             reads=[s_selm, s_bvT, s_big], writes=[s_vsel])
        p.op("vector", lambda e: e.tensor_reduce(out=s_ve[:], in_=s_vsel[:], axis=AX, op=ALU.min),
             reads=[s_vsel], writes=[s_ve], small=True)
        p.op("tensor", lambda e: e.transpose(out=p_vt[:], in_=s_ve[:], identity=s_idf[0:64, 0:64]),
             reads=[s_ve, s_idf], writes=[p_vt])
        p.op("scalar", lambda e: e.copy(out=s_vrow[:], in_=p_vt[:]), reads=[p_vt], writes=[s_vrow], small=True)
        p.op("tensor", lambda e: e.matmul(out=p_vb[:], lhsT=s_ones1[:], rhs=s_vrow[:],
                                          start=True, stop=True), reads=[s_ones1, s_vrow], writes=[p_vb])
        p.op("scalar", lambda e: e.copy(out=s_vb[:], in_=p_vb[:]), reads=[p_vb], writes=[s_vb])

        for tt in range(32):
            p.op("sync", lambda e, tt=tt: e.dma_start(out=s_sc[:], in_=scloc[tt * 128:(tt + 1) * 128, :]),
                 reads=["fsdone%d" % tt], writes=[s_sc], dma=True)
            p.op("vector", lambda e: e.tensor_tensor(out=s_msk[:], in0=s_sc[:], in1=s_vb[:], op=ALU.is_ge),
                 reads=[s_sc, s_vb], writes=[s_msk])
            p.op("vector", lambda e: e.tensor_mul(out=s_fs[:], in0=s_sc[:], in1=s_msk[:]),
                 reads=[s_sc, s_msk], writes=[s_fs])
            p.op("vector", lambda e: e.reduce_sum(out=s_rsum[:], in_=s_fs[:], axis=AX),
                 reads=[s_fs], writes=[s_rsum], small=True)
            p.op("vector", lambda e: e.tensor_scalar_add(out=s_rsum[:], in0=s_rsum[:], scalar1=1e-6),
                 reads=[s_rsum], writes=[s_rsum], small=True)
            p.op("vector", lambda e: e.reciprocal(out=s_rsum[:], in_=s_rsum[:]),
                 reads=[s_rsum], writes=[s_rsum], small=True)
            p.op("vector", lambda e: e.tensor_scalar(out=s_fs[:], in0=s_fs[:], scalar1=s_rsum[:, 0:1],
                                                     scalar2=None, op0=ALU.mult),
                 reads=[s_fs, s_rsum], writes=[s_fs])
            p.op("sync", lambda e, tt=tt: e.dma_start(out=fsloc[tt * 128:(tt + 1) * 128, :], in_=s_fs[:]),
                 reads=[s_fs], writes=["fsdone%d" % tt], dma=True)
        p.finalize()
    return nc


def build_ffn0_h():
    nc = bass.Bass("TRN2", target_bir_lowering=False)
    zTb = nc.dram_tensor("zTb", [B_LOC, D, N], BF16, kind="ExternalInput").ap()
    fsT = nc.dram_tensor("fsT", [64, 4096], F32, kind="ExternalInput").ap()
    Wup = nc.dram_tensor("Wup", [64, D, 256], F32, kind="ExternalInput").ap()
    Wgate = nc.dram_tensor("Wgate", [64, D, 256], F32, kind="ExternalInput").ap()
    bupT2 = nc.dram_tensor("bupT2", [2, 128, 64], F32, kind="ExternalInput").ap()
    bgateT2 = nc.dram_tensor("bgateT2", [2, 128, 64], F32, kind="ExternalInput").ap()
    ones1 = nc.dram_tensor("ones1", [1, 128], F32, kind="ExternalInput").ap()
    hbuf = nc.dram_tensor("hbuf", [128, 128, 4096], BF16, kind="ExternalOutput").ap()

    with ExitStack() as st:
        e = st.enter_context
        s_z = e(nc.sbuf_tensor("s_z", [128, 16 * N], BF16))
        s_wst = e(nc.sbuf_tensor("s_wst", [128, 2048], F32))
        s_wup = [e(nc.sbuf_tensor(f"s_wup{i}", [128, 2048], BF16)) for i in range(2)]
        s_wgt = [e(nc.sbuf_tensor(f"s_wgt{i}", [128, 2048], BF16)) for i in range(2)]
        s_bup = e(nc.sbuf_tensor("s_bup", [128, 128], F32))
        s_bgt = e(nc.sbuf_tensor("s_bgt", [128, 128], F32))
        s_fsst = e(nc.sbuf_tensor("s_fsst", [1, 4096], F32))
        s_fsb = e(nc.sbuf_tensor("s_fsb", [128, 512], F32))
        s_silu = e(nc.sbuf_tensor("s_silu", [128, 512], F32))
        s_gbf = e(nc.sbuf_tensor("s_gbf", [128, 512], F32))
        s_h = e(nc.sbuf_tensor("s_h", [128, 512], BF16))
        s_ones1 = e(nc.sbuf_tensor("s_ones1", [1, 128], F32))
        p_up = e(nc.psum_tensor("p_up", [128, 512], F32))
        p_fsb = e(nc.psum_tensor("p_fsb", [128, 512], F32))
        p = Prog(nc, st)

        dma(p, s_ones1[:], ones1, [], [s_ones1])
        for i0 in range(2):
            dma(p, s_bup[:, i0 * 64:(i0 + 1) * 64], bupT2[i0], [], [s_bup])
            dma(p, s_bgt[:, i0 * 64:(i0 + 1) * 64], bgateT2[i0], [], [s_bgt])

        def load_w(eidx, buf_i):
            wu = Wup[eidx].rearrange("(kt p) h -> p kt h", p=128)
            wg = Wgate[eidx].rearrange("(kt p) h -> p kt h", p=128)
            wstv = s_wst.ap().rearrange("p (kt h) -> p kt h", h=256)
            dma(p, wstv, wu, [s_wup[buf_i]], [s_wst])
            p.op("vector", lambda e, buf_i=buf_i: e.tensor_copy(out=s_wup[buf_i][:], in_=s_wst[:]),
                 reads=[s_wst], writes=[s_wup[buf_i]])
            dma(p, wstv, wg, [s_wgt[buf_i], s_wup[buf_i]], [s_wst])
            p.op("vector", lambda e, buf_i=buf_i: e.tensor_copy(out=s_wgt[buf_i][:], in_=s_wst[:]),
                 reads=[s_wst], writes=[s_wgt[buf_i]])

        load_w(0, 0)
        for b in range(B_LOC):
            for kt in range(8):
                dma(p, s_z.ap()[:, (b * 8 + kt) * N:(b * 8 + kt + 1) * N],
                    zTb[b, kt * 128:(kt + 1) * 128, :], [], [s_z])
        for ei in range(64):
            wb = ei % 2
            if ei + 1 < 64:
                load_w(ei + 1, (ei + 1) % 2)
            p.op("sync", lambda e, ei=ei: e.dma_start(out=s_fsst[0:1, :], in_=fsT[ei:ei + 1, :]),
                 reads=["fsb_used"], writes=[s_fsst], dma=True)
            for tg in range(8):
                b = tg // 4
                c0 = (tg % 4) * 512
                tglob = tg * 512
                p.op("tensor", lambda e, tglob=tglob: e.matmul(
                    out=p_fsb[:], lhsT=s_ones1[:], rhs=s_fsst[0:1, tglob:tglob + 512],
                    start=True, stop=True), reads=[s_ones1, s_fsst], writes=[p_fsb])
                p.op("scalar", lambda e: e.copy(out=s_fsb[:], in_=p_fsb[:]),
                     reads=[p_fsb], writes=[s_fsb, "fsb_used"])
                for i0 in range(2):
                    for kt in range(8):
                        p.op("tensor", lambda e, kt=kt, i0=i0, wb=wb, b=b, c0=c0: e.matmul(
                            out=p_up[:], lhsT=s_wup[wb][:, kt * 256 + i0 * 128: kt * 256 + i0 * 128 + 128],
                            rhs=s_z.ap()[:, (b * 8 + kt) * N + c0: (b * 8 + kt) * N + c0 + 512],
                            start=(kt == 0), stop=(kt == 7)),
                            reads=[s_wup[wb], s_z], writes=[p_up])
                    p.op("scalar", lambda e, i0=i0, ei=ei: e.activation(
                        out=s_silu[:], in_=p_up[:], func=AF.Silu,
                        bias=s_bup[:, i0 * 64 + ei: i0 * 64 + ei + 1]),
                        reads=[p_up, s_bup], writes=[s_silu])
                    for kt in range(8):
                        p.op("tensor", lambda e, kt=kt, i0=i0, wb=wb, b=b, c0=c0: e.matmul(
                            out=p_up[:], lhsT=s_wgt[wb][:, kt * 256 + i0 * 128: kt * 256 + i0 * 128 + 128],
                            rhs=s_z.ap()[:, (b * 8 + kt) * N + c0: (b * 8 + kt) * N + c0 + 512],
                            start=(kt == 0), stop=(kt == 7)),
                            reads=[s_wgt[wb], s_z], writes=[p_up])
                    p.op("vector", lambda e, i0=i0, ei=ei: e.scalar_tensor_tensor(
                        out=s_gbf[:], in0=p_up[:], scalar=s_bgt[:, i0 * 64 + ei: i0 * 64 + ei + 1],
                        in1=s_fsb[:], op0=ALU.add, op1=ALU.mult),
                        reads=[p_up, s_bgt, s_fsb], writes=[s_gbf])
                    p.op("vector", lambda e: e.tensor_mul(out=s_h[:], in0=s_gbf[:], in1=s_silu[:]),
                         reads=[s_gbf, s_silu], writes=[s_h])
                    dma(p, hbuf[:, ei * 2 + i0, tglob:tglob + 512], s_h[:], [s_h], ["hout"])
        p.finalize()
    return nc


def build_ffn0_down():
    nc = bass.Bass("TRN2", target_bir_lowering=False)
    hbuf = nc.dram_tensor("hbuf", [128, 128, 4096], BF16, kind="ExternalInput").ap()
    wdncat = nc.dram_tensor("wdncat", [16384, D], F32, kind="ExternalInput").ap()
    bdown = nc.dram_tensor("bdown", [64, D], F32, kind="ExternalInput").ap()
    fsT = nc.dram_tensor("fsT", [64, 4096], F32, kind="ExternalInput").ap()
    y1T = nc.dram_tensor("y1T", [B_LOC, D, N], F32, kind="ExternalInput").ap()
    y2T = nc.dram_tensor("y2T", [B_LOC, D, N], F32, kind="ExternalOutput").ap()

    with ExitStack() as st:
        e = st.enter_context
        s_h = e(nc.sbuf_tensor("s_h", [128, 128 * 512], BF16))
        s_wst = e(nc.sbuf_tensor("s_wst", [128, D], F32))
        s_wdn = [e(nc.sbuf_tensor(f"s_wdn{i}", [128, D], BF16)) for i in range(2)]
        s_bdn = e(nc.sbuf_tensor("s_bdn", [64, D], F32))
        s_fsT = e(nc.sbuf_tensor("s_fsT", [64, 4096], F32))
        s_y1 = e(nc.sbuf_tensor("s_y1", [128, 512], F32))
        s_o = e(nc.sbuf_tensor("s_o", [128, 512], F32))
        p_dn = [e(nc.psum_tensor(f"p_dn{j}", [128, 512], F32)) for j in range(7)]
        p_dn7 = e(nc.psum_tensor("p_dn7", [128, 512], F32))
        p = Prog(nc, st)

        dma(p, s_bdn[:], bdown, [], [s_bdn])
        dma(p, s_fsT[:], fsT, [], [s_fsT])

        def load_wdn(ktg, buf_i):
            dma(p, s_wst[:], wdncat[ktg * 128:(ktg + 1) * 128, :], [s_wdn[buf_i]], [s_wst])
            p.op("vector", lambda e, buf_i=buf_i: e.tensor_copy(out=s_wdn[buf_i][:], in_=s_wst[:]),
                 reads=[s_wst], writes=[s_wdn[buf_i]])

        for tg in range(8):
            c0 = tg * 512
            b = tg // 4
            cb = (tg % 4) * 512
            for ktg in range(128):
                dma(p, s_h.ap()[:, ktg * 512:(ktg + 1) * 512], hbuf[:, ktg, c0:c0 + 512],
                    ["hused"], [s_h])
            load_wdn(0, 0)
            for ktg in range(128):
                wb = ktg % 2
                if ktg + 1 < 128:
                    load_wdn(ktg + 1, (ktg + 1) % 2)
                for j in range(8):
                    pd = p_dn[j] if j < 7 else p_dn7
                    p.op("tensor", lambda e, j=j, ktg=ktg, wb=wb, pd=pd: e.matmul(
                        out=pd[:], lhsT=s_wdn[wb][:, j * 128:(j + 1) * 128],
                        rhs=s_h.ap()[:, ktg * 512:(ktg + 1) * 512],
                        start=(ktg == 0), stop=(ktg == 127)),
                        reads=[s_wdn[wb], s_h], writes=[pd])
            p.op("vector", lambda e: e.engine_nop(), reads=[s_h], writes=["hused"])
            for j in range(8):
                pd = p_dn[j] if j < 7 else p_dn7
                for kt in range(8):
                    pass
                dma(p, s_y1[:], y1T[b, j * 128:(j + 1) * 128, cb:cb + 512], [s_y1, "y2w"], [s_y1])
                p.op("vector", lambda e, pd=pd: e.tensor_add(out=s_o[:], in0=s_y1[:], in1=pd[:]),
                     reads=[s_y1, pd], writes=[s_o])
                p.op("tensor", lambda e, j=j, c0=c0, pd=pd: e.matmul(
                    out=pd[:], lhsT=s_bdn[:, j * 128:(j + 1) * 128],
                    rhs=s_fsT[:, c0:c0 + 512], start=True, stop=True),
                    reads=[s_bdn, s_fsT, s_o], writes=[pd])
                p.op("vector", lambda e, pd=pd: e.tensor_add(out=s_o[:], in0=s_o[:], in1=pd[:]),
                     reads=[s_o, pd], writes=[s_o])
                p.op("sync", lambda e, b=b, j=j, cb=cb: e.dma_start(
                    out=y2T[b, j * 128:(j + 1) * 128, cb:cb + 512], in_=s_o[:]),
                    reads=[s_o], writes=["y2w"], dma=True)
        p.finalize()
    return nc


DI = 2048


def build_ssm_pre():
    nc = bass.Bass("TRN2", target_bir_lowering=False)
    y2T = nc.dram_tensor("y2T", [B_LOC, D, N], F32, kind="ExternalInput").ap()
    WinT = nc.dram_tensor("WinT", [D, 2 * DI], F32, kind="ExternalInput").ap()
    WdtT = nc.dram_tensor("WdtT", [64, DI], F32, kind="ExternalInput").ap()
    WBT = nc.dram_tensor("WBT", [D, 16], F32, kind="ExternalInput").ap()
    WCT = nc.dram_tensor("WCT", [D, 16], F32, kind="ExternalInput").ap()
    cwcol = nc.dram_tensor("cwcol", [128, 16, 4], F32, kind="ExternalInput").ap()
    cbcol = nc.dram_tensor("cbcol", [128, 16], F32, kind="ExternalInput").ap()
    bdtcol = nc.dram_tensor("bdtcol", [128, 16], F32, kind="ExternalInput").ap()
    ones1 = nc.dram_tensor("ones1", [1, 128], F32, kind="ExternalInput").ap()
    onesc = nc.dram_tensor("onesc", [128, 1], F32, kind="ExternalInput").ap()
    xvT = nc.dram_tensor("xvT", [B_LOC, DI, N], F32, kind="ExternalOutput").ap()
    zsT = nc.dram_tensor("zsT", [B_LOC, DI, N], F32, kind="ExternalOutput").ap()
    dtT = nc.dram_tensor("dtT", [B_LOC, DI, N], F32, kind="ExternalOutput").ap()
    BT = nc.dram_tensor("BT", [B_LOC, 16, N], F32, kind="ExternalOutput").ap()
    CT = nc.dram_tensor("CT", [B_LOC, 16, N], F32, kind="ExternalOutput").ap()

    with ExitStack() as st:
        e = st.enter_context
        s_y = e(nc.sbuf_tensor("s_y", [128, 8 * GRP], F32))
        s_x1 = e(nc.sbuf_tensor("s_x1", [128, 8 * GRP], F32))
        s_x1b = e(nc.sbuf_tensor("s_x1b", [128, 8 * GRP], BF16))
        s_zsq = e(nc.sbuf_tensor("s_zsq", [128, GRP], F32))
        s_mu = e(nc.sbuf_tensor("s_mu", [1, GRP], F32))
        s_m2 = e(nc.sbuf_tensor("s_m2", [1, GRP], F32))
        s_var = e(nc.sbuf_tensor("s_var", [1, GRP], F32))
        s_rstd = e(nc.sbuf_tensor("s_rstd", [1, GRP], F32))
        s_win = e(nc.sbuf_tensor("s_win", [128, 8 * 4096], BF16))
        s_wst = e(nc.sbuf_tensor("s_wst", [128, 4096], F32))
        s_wdt = e(nc.sbuf_tensor("s_wdt", [64, DI], BF16))
        s_wb = e(nc.sbuf_tensor("s_wb", [128, 16], F32))
        s_wc = e(nc.sbuf_tensor("s_wc", [128, 16], F32))
        s_cw = e(nc.sbuf_tensor("s_cw", [128, 64], F32))
        s_cb = e(nc.sbuf_tensor("s_cb", [128, 16], F32))
        s_bdt = e(nc.sbuf_tensor("s_bdt", [128, 16], F32))
        s_ones1 = e(nc.sbuf_tensor("s_ones1", [1, 128], F32))
        s_onesc = e(nc.sbuf_tensor("s_onesc", [128, 1], F32))
        s_ci = e(nc.sbuf_tensor("s_ci", [128, GRP + 3], F32))
        s_tail = e(nc.sbuf_tensor("s_tail", [128, 16 * 3], F32))
        s_acc = e(nc.sbuf_tensor("s_acc", [128, GRP], F32))
        s_xvs = e(nc.sbuf_tensor("s_xvs", [128, GRP], F32))
        s_t1 = e(nc.sbuf_tensor("s_t1", [128, GRP], F32))
        s_t2 = e(nc.sbuf_tensor("s_t2", [128, GRP], F32))
        s_dt = e(nc.sbuf_tensor("s_dt", [128, GRP], F32))
        s_bc = e(nc.sbuf_tensor("s_bc", [16, GRP], F32))
        p_xz = e(nc.psum_tensor("p_xz", [128, GRP], F32))
        p_stat = e(nc.psum_tensor("p_stat", [1, GRP], F32))
        p_bcast = e(nc.psum_tensor("p_bcast", [128, GRP], F32))
        p_dt = e(nc.psum_tensor("p_dt", [128, GRP], F32))
        p_bc = e(nc.psum_tensor("p_bc", [16, GRP], F32))
        p = Prog(nc, st)

        dma(p, s_ones1[:], ones1, [], [s_ones1])
        dma(p, s_onesc[:], onesc, [], [s_onesc])
        dma(p, s_cw.ap().rearrange("p (kt k) -> p kt k", k=4), cwcol, [], [s_cw])
        dma(p, s_cb[:], cbcol, [], [s_cb])
        dma(p, s_bdt[:], bdtcol, [], [s_bdt])
        dma(p, s_wst[0:64, 0:DI], WdtT, [s_wdt], [s_wst])
        p.op("vector", lambda e: e.tensor_copy(out=s_wdt[:], in_=s_wst[0:64, 0:DI]),
             reads=[s_wst], writes=[s_wdt])
        dma(p, s_wb[0:128, 0:16], WBT[0:128, :], [], ["wbt_part"])
        for kt in range(8):
            dma(p, s_wst[:], WinT[kt * 128:(kt + 1) * 128, :], [s_win], [s_wst])
            p.op("vector", lambda e, kt=kt: e.tensor_copy(
                out=s_win[:, kt * 4096:(kt + 1) * 4096], in_=s_wst[:]),
                reads=[s_wst], writes=[s_win])
        s_wbt = e(nc.sbuf_tensor("s_wbt", [128, 8 * 16], F32))
        s_wct = e(nc.sbuf_tensor("s_wct", [128, 8 * 16], F32))
        dma(p, s_wbt.ap().rearrange("p (kt s) -> p kt s", s=16), WBT.rearrange("(kt p) s -> p kt s", p=128), [], [s_wbt])
        dma(p, s_wct.ap().rearrange("p (kt s) -> p kt s", s=16), WCT.rearrange("(kt p) s -> p kt s", p=128), [], [s_wct])

        def ln_group(b, g):
            for kt in range(8):
                dma(p, s_y.ap()[:, kt * GRP:(kt + 1) * GRP],
                    y2T[b, kt * 128:(kt + 1) * 128, g * GRP:(g + 1) * GRP], [], [s_y])
            sa_ = s_y.ap()
            for kt in range(8):
                p.op("tensor", lambda e, kt=kt: e.matmul(
                    out=p_stat[:], lhsT=s_onesc[:], rhs=sa_[:, kt * GRP:(kt + 1) * GRP],
                    start=(kt == 0), stop=(kt == 7)), reads=[s_onesc, s_y], writes=[p_stat])
            p.op("scalar", lambda e: e.activation(out=s_mu[:], in_=p_stat[:], func=AF.Copy, scale=1.0 / D),
                 reads=[p_stat], writes=[s_mu], small=True)
            for kt in range(8):
                p.op("scalar", lambda e, kt=kt: e.activation(
                    out=s_zsq[:], in_=sa_[:, kt * GRP:(kt + 1) * GRP], func=AF.Square),
                    reads=[s_y], writes=[s_zsq])
                p.op("tensor", lambda e, kt=kt: e.matmul(
                    out=p_stat[:], lhsT=s_onesc[:], rhs=s_zsq[:],
                    start=(kt == 0), stop=(kt == 7)), reads=[s_onesc, s_zsq], writes=[p_stat])
            p.op("scalar", lambda e: e.activation(out=s_m2[:], in_=p_stat[:], func=AF.Copy, scale=1.0 / D),
                 reads=[p_stat], writes=[s_m2], small=True)
            p.op("vector", lambda e: e.tensor_mul(out=s_var[:], in0=s_mu[:], in1=s_mu[:]),
                 reads=[s_mu], writes=[s_var], small=True)
            p.op("vector", lambda e: e.tensor_sub(out=s_var[:], in0=s_m2[:], in1=s_var[:]),
                 reads=[s_m2, s_var], writes=[s_var], small=True)
            p.op("vector", lambda e: e.tensor_scalar_add(out=s_var[:], in0=s_var[:], scalar1=1e-5),
                 reads=[s_var], writes=[s_var], small=True)
            p.op("scalar", lambda e: e.activation(out=s_rstd[:], in_=s_var[:], func=AF.Ln),
                 reads=[s_var], writes=[s_rstd], small=True)
            p.op("scalar", lambda e: e.activation(out=s_rstd[:], in_=s_rstd[:], func=AF.Exp, scale=-0.5),
                 reads=[s_rstd], writes=[s_rstd], small=True)
            p.op("tensor", lambda e: e.matmul(out=p_bcast[:], lhsT=s_ones1[:], rhs=s_mu[:],
                                              start=True, stop=True), reads=[s_ones1, s_mu], writes=[p_bcast])
            for kt in range(8):
                p.op("vector", lambda e, kt=kt: e.tensor_sub(
                    out=s_x1.ap()[:, kt * GRP:(kt + 1) * GRP], in0=sa_[:, kt * GRP:(kt + 1) * GRP],
                    in1=p_bcast[:]), reads=[s_y, p_bcast], writes=[s_x1])
            p.op("tensor", lambda e: e.matmul(out=p_bcast[:], lhsT=s_ones1[:], rhs=s_rstd[:],
                                              start=True, stop=True), reads=[s_ones1, s_rstd], writes=[p_bcast])
            for kt in range(8):
                p.op("vector", lambda e, kt=kt: e.tensor_mul(
                    out=s_x1.ap()[:, kt * GRP:(kt + 1) * GRP], in0=s_x1.ap()[:, kt * GRP:(kt + 1) * GRP],
                    in1=p_bcast[:]), reads=[s_x1, p_bcast], writes=[s_x1])
            for kt in range(8):
                p.op("vector", lambda e, kt=kt: e.tensor_copy(
                    out=s_x1b.ap()[:, kt * GRP:(kt + 1) * GRP], in_=s_x1.ap()[:, kt * GRP:(kt + 1) * GRP]),
                    reads=[s_x1], writes=[s_x1b])

        for b in range(B_LOC):
            p.op("vector", lambda e: e.memset(s_tail[:], 0.0), reads=[], writes=[s_tail])
            for g in range(NGRP):
                c0 = g * GRP
                ln_group(b, g)
                for fi in range(32):
                    for kt in range(8):
                        p.op("tensor", lambda e, fi=fi, kt=kt: e.matmul(
                            out=p_xz[:], lhsT=s_win[:, kt * 4096 + fi * 128: kt * 4096 + (fi + 1) * 128],
                            rhs=s_x1b.ap()[:, kt * GRP:(kt + 1) * GRP],
                            start=(kt == 0), stop=(kt == 7)), reads=[s_win, s_x1b], writes=[p_xz])
                    if fi >= 16:
                        p.op("scalar", lambda e: e.copy(out=s_t2[:], in_=p_xz[:]),
                             reads=[p_xz], writes=[s_t2])
                        dma(p, zsT[b, (fi - 16) * 128:(fi - 15) * 128, c0:c0 + GRP], s_t2[:],
                            [s_t2], ["zs_out"])
                    else:
                        kt_ = fi
                        p.op("vector", lambda e, kt_=kt_: e.tensor_copy(
                            out=s_ci[:, 0:3], in_=s_tail.ap()[:, kt_ * 3:(kt_ + 1) * 3]),
                            reads=[s_tail], writes=[s_ci])
                        p.op("scalar", lambda e: e.copy(out=s_ci[:, 3:GRP + 3], in_=p_xz[:]),
                             reads=[p_xz], writes=[s_ci])
                        p.op("vector", lambda e, kt_=kt_: e.tensor_copy(
                            out=s_tail.ap()[:, kt_ * 3:(kt_ + 1) * 3], in_=s_ci[:, GRP:GRP + 3]),
                            reads=[s_ci], writes=[s_tail])
                        p.op("vector", lambda e, kt_=kt_: e.tensor_scalar(
                            out=s_acc[:], in0=s_ci[:, 0:GRP], scalar1=s_cw[:, kt_ * 4:kt_ * 4 + 1],
                            scalar2=None, op0=ALU.mult), reads=[s_ci, s_cw], writes=[s_acc])
                        for k in range(1, 4):
                            p.op("vector", lambda e, kt_=kt_, k=k: e.scalar_tensor_tensor(
                                out=s_acc[:], in0=s_ci[:, k:k + GRP], scalar=s_cw[:, kt_ * 4 + k:kt_ * 4 + k + 1],
                                in1=s_acc[:], op0=ALU.mult, op1=ALU.add),
                                reads=[s_ci, s_cw, s_acc], writes=[s_acc])
                        p.op("scalar", lambda e, kt_=kt_: e.activation(
                            out=s_xvs[:], in_=s_acc[:], func=AF.Silu, bias=s_cb[:, kt_:kt_ + 1]),
                            reads=[s_acc, s_cb], writes=[s_xvs])
                        dma(p, xvT[b, kt_ * 128:(kt_ + 1) * 128, c0:c0 + GRP], s_xvs[:],
                            [s_xvs], ["xv_out"])
                for fi in range(16):
                    p.op("tensor", lambda e, fi=fi: e.matmul(
                        out=p_dt[:], lhsT=s_wdt[:, fi * 128:(fi + 1) * 128],
                        rhs=s_x1b.ap()[0:64, 0:GRP],
                        start=True, stop=True), reads=[s_wdt, s_x1b], writes=[p_dt])
                    p.op("vector", lambda e, fi=fi: e.tensor_scalar(
                        out=s_t1[:], in0=p_dt[:], scalar1=s_bdt[:, fi:fi + 1], scalar2=None,
                        op0=ALU.add), reads=[p_dt, s_bdt], writes=[s_t1])
                    p.op("scalar", lambda e: e.activation(out=s_t2[:], in_=s_t1[:], func=AF.Abs),
                         reads=[s_t1], writes=[s_t2])
                    p.op("scalar", lambda e: e.activation(out=s_t2[:], in_=s_t2[:], func=AF.Exp, scale=-1.0),
                         reads=[s_t2], writes=[s_t2])
                    p.op("scalar", lambda e: e.activation(out=s_t2[:], in_=s_t2[:], func=AF.Ln, bias=1.0),
                         reads=[s_t2], writes=[s_t2])
                    p.op("vector", lambda e: e.tensor_relu(out=s_t1[:], in_=s_t1[:]),
                         reads=[s_t1], writes=[s_t1])
                    p.op("vector", lambda e: e.tensor_add(out=s_dt[:], in0=s_t1[:], in1=s_t2[:]),
                         reads=[s_t1, s_t2], writes=[s_dt])
                    dma(p, dtT[b, fi * 128:(fi + 1) * 128, c0:c0 + GRP], s_dt[:],
                        [s_dt], ["dt_out"])
                for which, wt, dst in (("B", s_wbt, BT), ("C", s_wct, CT)):
                    for kt in range(8):
                        p.op("tensor", lambda e, kt=kt, wt=wt: e.matmul(
                            out=p_bc[:], lhsT=wt[:, kt * 16:(kt + 1) * 16],
                            rhs=s_x1.ap()[:, kt * GRP:(kt + 1) * GRP],
                            start=(kt == 0), stop=(kt == 7)), reads=[wt, s_x1], writes=[p_bc])
                    p.op("scalar", lambda e: e.copy(out=s_bc[:], in_=p_bc[:]),
                         reads=[p_bc], writes=[s_bc])
                    dma(p, dst[b, :, c0:c0 + GRP], s_bc[:], [s_bc], ["bc_out"])
        p.finalize()
    return nc


def build_scan():
    nc = bass.Bass("TRN2", target_bir_lowering=False)
    xvT = nc.dram_tensor("xvT", [B_LOC, DI, N], F32, kind="ExternalInput").ap()
    dtT = nc.dram_tensor("dtT", [B_LOC, DI, N], F32, kind="ExternalInput").ap()
    BT = nc.dram_tensor("BT", [B_LOC, 16, N], F32, kind="ExternalInput").ap()
    CT = nc.dram_tensor("CT", [B_LOC, 16, N], F32, kind="ExternalInput").ap()
    Acols = nc.dram_tensor("Acols", [128, 256], F32, kind="ExternalInput").ap()
    sel = nc.dram_tensor("sel", [128, 16, 128], F32, kind="ExternalInput").ap()
    repB = nc.dram_tensor("repB", [16, 128], F32, kind="ExternalInput").ap()
    red8 = nc.dram_tensor("red8", [128, 8], F32, kind="ExternalInput").ap()
    yT = nc.dram_tensor("yT", [B_LOC, DI, N], F32, kind="ExternalOutput").ap()

    with ExitStack() as st:
        e = st.enter_context
        s_A = e(nc.sbuf_tensor("s_A", [128, 256], F32))
        s_sel = e(nc.sbuf_tensor("s_sel", [128, 16 * 128], F32))
        s_repB = e(nc.sbuf_tensor("s_repB", [16, 128], F32))
        s_red8 = e(nc.sbuf_tensor("s_red8", [128, 8], F32))
        s_BT = e(nc.sbuf_tensor("s_BT", [16, N], F32))
        s_CT = e(nc.sbuf_tensor("s_CT", [16, N], F32))
        s_Brep = e(nc.sbuf_tensor("s_Brep", [128, N], F32))
        s_Crep = e(nc.sbuf_tensor("s_Crep", [128, N], F32))
        s_dtk = [e(nc.sbuf_tensor(f"s_dtk{i}", [128, N], F32)) for i in range(2)]
        s_xvk = [e(nc.sbuf_tensor(f"s_xvk{i}", [128, N], F32)) for i in range(2)]
        s_wk = e(nc.sbuf_tensor("s_wk", [128, N], F32))
        s_dA = e(nc.sbuf_tensor("s_dA", [128, N], F32))
        s_u = e(nc.sbuf_tensor("s_u", [128, N], F32))
        s_hh = e(nc.sbuf_tensor("s_hh", [128, N], F32))
        s_yc = e(nc.sbuf_tensor("s_yc", [128, N], F32))
        s_y8 = e(nc.sbuf_tensor("s_y8", [8, N], F32))
        p_rep = e(nc.psum_tensor("p_rep", [128, 512], F32))
        p_rep2 = e(nc.psum_tensor("p_rep2", [128, 512], F32))
        p_br = e(nc.psum_tensor("p_br", [128, 512], F32))
        p_y = e(nc.psum_tensor("p_y", [8, N], F32))
        p = Prog(nc, st)

        dma(p, s_A[:], Acols, [], [s_A])
        dma(p, s_sel.ap().rearrange("p (jj i) -> p jj i", i=128), sel, [], [s_sel])
        dma(p, s_repB[:], repB, [], [s_repB])
        dma(p, s_red8[:], red8, [], [s_red8])
        p.op("scalar", lambda e: e.activation(out=s_A[:], in_=s_A[:], func=AF.Exp),
             reads=[s_A], writes=[s_A])
        p.op("vector", lambda e: e.tensor_scalar_mul(out=s_A[:], in0=s_A[:], scalar1=-1.0),
             reads=[s_A], writes=[s_A])

        for b in range(B_LOC):
            dma(p, s_BT[:], BT[b], ["brep_done"], [s_BT])
            dma(p, s_CT[:], CT[b], ["crep_done"], [s_CT])
            for which, src, dst, done in (("B", s_BT, s_Brep, "brep_done"), ("C", s_CT, s_Crep, "crep_done")):
                for c in range(4):
                    p.op("tensor", lambda e, c=c, src=src: e.matmul(
                        out=p_br[:], lhsT=s_repB[:], rhs=src[:, c * 512:(c + 1) * 512],
                        start=True, stop=True), reads=[s_repB, src], writes=[p_br])
                    p.op("scalar", lambda e, c=c, dst=dst: e.copy(
                        out=dst[:, c * 512:(c + 1) * 512], in_=p_br[:]),
                        reads=[p_br], writes=[dst, done])
            for kt in range(16):
                kb = kt % 2
                dma(p, s_dtk[kb][:], dtT[b, kt * 128:(kt + 1) * 128, :], [s_dtk[kb]], [s_dtk[kb]])
                dma(p, s_xvk[kb][:], xvT[b, kt * 128:(kt + 1) * 128, :], [s_xvk[kb]], [s_xvk[kb]])
                p.op("vector", lambda e, kb=kb: e.tensor_mul(out=s_wk[:], in0=s_dtk[kb][:], in1=s_xvk[kb][:]),
                     reads=[s_dtk[kb], s_xvk[kb], s_wk], writes=[s_wk])
                for jj in range(16):
                    tile = kt * 16 + jj
                    for c in range(4):
                        p.op("tensor", lambda e, jj=jj, c=c, kb=kb: e.matmul(
                            out=p_rep[:], lhsT=s_sel[:, jj * 128:(jj + 1) * 128],
                            rhs=s_dtk[kb][:, c * 512:(c + 1) * 512],
                            start=True, stop=True), reads=[s_sel, s_dtk[kb]], writes=[p_rep])
                        p.op("scalar", lambda e, c=c, tile=tile: e.activation(
                            out=s_dA[:, c * 512:(c + 1) * 512], in_=p_rep[:], func=AF.Exp,
                            scale=s_A[:, tile:tile + 1]), reads=[p_rep, s_A], writes=[s_dA])
                        p.op("tensor", lambda e, jj=jj, c=c: e.matmul(
                            out=p_rep2[:], lhsT=s_sel[:, jj * 128:(jj + 1) * 128],
                            rhs=s_wk[:, c * 512:(c + 1) * 512],
                            start=True, stop=True), reads=[s_sel, s_wk], writes=[p_rep2])
                        p.op("vector", lambda e, c=c: e.tensor_mul(
                            out=s_u[:, c * 512:(c + 1) * 512], in0=p_rep2[:],
                            in1=s_Brep[:, c * 512:(c + 1) * 512]),
                            reads=[p_rep2, s_Brep], writes=[s_u])
                    p.op("vector", lambda e: e.tensor_tensor_scan(
                        out=s_hh[:], data0=s_dA[:], data1=s_u[:], initial=0.0,
                        op0=ALU.mult, op1=ALU.add), reads=[s_dA, s_u], writes=[s_hh])
                    p.op("vector", lambda e: e.tensor_mul(out=s_yc[:], in0=s_hh[:], in1=s_Crep[:]),
                         reads=[s_hh, s_Crep], writes=[s_yc])
                    for c in range(4):
                        p.op("tensor", lambda e, c=c: e.matmul(
                            out=p_y[:, c * 512:(c + 1) * 512], lhsT=s_red8[:],
                            rhs=s_yc[:, c * 512:(c + 1) * 512],
                            start=True, stop=True), reads=[s_red8, s_yc], writes=[p_y])
                    p.op("scalar", lambda e: e.copy(out=s_y8[:], in_=p_y[:]),
                         reads=[p_y], writes=[s_y8])
                    dma(p, yT[b, kt * 128 + jj * 8: kt * 128 + (jj + 1) * 8, :], s_y8[:],
                        [s_y8], ["y_out"])
        p.finalize()
    return nc


def build_ssm_post():
    nc = bass.Bass("TRN2", target_bir_lowering=False)
    yT = nc.dram_tensor("yT", [B_LOC, DI, N], F32, kind="ExternalInput").ap()
    zsT = nc.dram_tensor("zsT", [B_LOC, DI, N], F32, kind="ExternalInput").ap()
    y2T = nc.dram_tensor("y2T", [B_LOC, D, N], F32, kind="ExternalInput").ap()
    WoutT = nc.dram_tensor("WoutT", [DI, D], F32, kind="ExternalInput").ap()
    y3T = nc.dram_tensor("y3T", [B_LOC, D, N], F32, kind="ExternalOutput").ap()

    with ExitStack() as st:
        e = st.enter_context
        s_wst = e(nc.sbuf_tensor("s_wst", [128, D], F32))
        s_wo = e(nc.sbuf_tensor("s_wo", [128, 16 * D], BF16))
        s_yk = e(nc.sbuf_tensor("s_yk", [128, GRP], F32))
        s_zk = e(nc.sbuf_tensor("s_zk", [128, GRP], F32))
        s_g = e(nc.sbuf_tensor("s_g", [128, 16 * GRP], BF16))
        s_y2 = e(nc.sbuf_tensor("s_y2", [128, GRP], F32))
        s_o = e(nc.sbuf_tensor("s_o", [128, GRP], F32))
        p_o = e(nc.psum_tensor("p_o", [128, GRP], F32))
        p = Prog(nc, st)

        for kt in range(16):
            dma(p, s_wst[:], WoutT[kt * 128:(kt + 1) * 128, :], [s_wo], [s_wst])
            p.op("vector", lambda e, kt=kt: e.tensor_copy(
                out=s_wo[:, kt * D:(kt + 1) * D], in_=s_wst[:]),
                reads=[s_wst], writes=[s_wo])
        for b in range(B_LOC):
            for g in range(NGRP):
                c0 = g * GRP
                for kt in range(16):
                    dma(p, s_yk[:], yT[b, kt * 128:(kt + 1) * 128, c0:c0 + GRP], [s_yk], [s_yk])
                    dma(p, s_zk[:], zsT[b, kt * 128:(kt + 1) * 128, c0:c0 + GRP], [s_zk], [s_zk])
                    p.op("scalar", lambda e: e.activation(out=s_zk[:], in_=s_zk[:], func=AF.Silu),
                         reads=[s_zk], writes=[s_zk])
                    p.op("vector", lambda e, kt=kt: e.tensor_mul(
                        out=s_g.ap()[:, kt * GRP:(kt + 1) * GRP], in0=s_yk[:], in1=s_zk[:]),
                        reads=[s_yk, s_zk], writes=[s_g])
                for j in range(8):
                    for kt in range(16):
                        p.op("tensor", lambda e, j=j, kt=kt: e.matmul(
                            out=p_o[:], lhsT=s_wo[:, kt * D + j * 128: kt * D + (j + 1) * 128],
                            rhs=s_g.ap()[:, kt * GRP:(kt + 1) * GRP],
                            start=(kt == 0), stop=(kt == 15)), reads=[s_wo, s_g], writes=[p_o])
                    dma(p, s_y2[:], y2T[b, j * 128:(j + 1) * 128, c0:c0 + GRP], [s_y2, "y3w"], [s_y2])
                    p.op("vector", lambda e: e.tensor_add(out=s_o[:], in0=s_y2[:], in1=p_o[:]),
                         reads=[s_y2, p_o], writes=[s_o])
                    p.op("sync", lambda e, b=b, j=j, c0=c0: e.dma_start(
                        out=y3T[b, j * 128:(j + 1) * 128, c0:c0 + GRP], in_=s_o[:]),
                        reads=[s_o], writes=["y3w"], dma=True)
        p.finalize()
    return nc


def build_ffn1():
    nc = bass.Bass("TRN2", target_bir_lowering=False)
    y3T = nc.dram_tensor("y3T", [B_LOC, D, N], F32, kind="ExternalInput").ap()
    sigsrawT = nc.dram_tensor("sigsrawT", [D, 64], F32, kind="ExternalInput").ap()
    knots = nc.dram_tensor("knots", [1, 8], F32, kind="ExternalInput").ap()
    temp = nc.dram_tensor("temp", [1, 1], F32, kind="ExternalInput").ap()
    cw1 = nc.dram_tensor("cw1", [D, 256], F32, kind="ExternalInput").ap()
    cb1col = nc.dram_tensor("cb1col", [128, 2], F32, kind="ExternalInput").ap()
    cw2 = nc.dram_tensor("cw2", [256, 2], F32, kind="ExternalInput").ap()
    cb2col = nc.dram_tensor("cb2col", [2, 1], F32, kind="ExternalInput").ap()
    spcat = nc.dram_tensor("spcat", [64, 65], F32, kind="ExternalInput").ap()
    dirs = nc.dram_tensor("dirs", [64, D], F32, kind="ExternalInput").ap()
    gcol = nc.dram_tensor("gcol", [128, 8], F32, kind="ExternalInput").ap()
    bcol = nc.dram_tensor("bcol", [128, 8], F32, kind="ExternalInput").ap()
    osc = nc.dram_tensor("osc", [1, 1], F32, kind="ExternalInput").ap()
    identf = nc.dram_tensor("identf", [128, 128], F32, kind="ExternalInput").ap()
    ones1 = nc.dram_tensor("ones1", [1, 128], F32, kind="ExternalInput").ap()
    onesc = nc.dram_tensor("onesc", [128, 1], F32, kind="ExternalInput").ap()
    outT = nc.dram_tensor("outT", [B_LOC, D, N], F32, kind="ExternalOutput").ap()

    with ExitStack() as st:
        e = st.enter_context
        s_y = e(nc.sbuf_tensor("s_y", [128, 8 * GRP], F32))
        s_xf = e(nc.sbuf_tensor("s_xf", [128, 8 * GRP], F32))
        s_xfb = e(nc.sbuf_tensor("s_xfb", [128, 8 * GRP], BF16))
        s_zsq = e(nc.sbuf_tensor("s_zsq", [128, GRP], F32))
        s_mu = e(nc.sbuf_tensor("s_mu", [1, GRP], F32))
        s_m2 = e(nc.sbuf_tensor("s_m2", [1, GRP], F32))
        s_var = e(nc.sbuf_tensor("s_var", [1, GRP], F32))
        s_rstd = e(nc.sbuf_tensor("s_rstd", [1, GRP], F32))
        s_sraw = e(nc.sbuf_tensor("s_sraw", [128, 8 * 64], F32))
        s_q = e(nc.sbuf_tensor("s_q", [128, 8 * 64], F32))
        s_cs = e(nc.sbuf_tensor("s_cs", [128, 8 * 8], F32))
        s_t8 = e(nc.sbuf_tensor("s_t8", [128, 8], F32))
        s_kn = e(nc.sbuf_tensor("s_kn", [128, 8], F32))
        s_dk = e(nc.sbuf_tensor("s_dk", [128, 7], F32))
        s_rt = e(nc.sbuf_tensor("s_rt", [128, 1], F32))
        s_nrt = e(nc.sbuf_tensor("s_nrt", [128, 1], F32))
        s_osc = e(nc.sbuf_tensor("s_osc", [128, 1], F32))
        s_r1 = e(nc.sbuf_tensor("s_r1", [1, 1], F32))
        s_cw1 = e(nc.sbuf_tensor("s_cw1", [128, 8 * 256], BF16))
        s_cw2 = e(nc.sbuf_tensor("s_cw2", [128, 2 * 2], BF16))
        s_cb1 = e(nc.sbuf_tensor("s_cb1", [128, 2], F32))
        s_cb2 = e(nc.sbuf_tensor("s_cb2", [2, 1], F32))
        s_wst = e(nc.sbuf_tensor("s_wst", [128, 256], F32))
        s_hc = e(nc.sbuf_tensor("s_hc", [128, 2 * GRP], BF16))
        s_comp = e(nc.sbuf_tensor("s_comp", [2, GRP], F32))
        s_compT = e(nc.sbuf_tensor("s_compT", [128, 8], F32))
        s_spcat = e(nc.sbuf_tensor("s_spcat", [64, 65], F32))
        s_dirs = e(nc.sbuf_tensor("s_dirs", [64, D], F32))
        s_gcol = e(nc.sbuf_tensor("s_gcol", [128, 8], F32))
        s_bcol = e(nc.sbuf_tensor("s_bcol", [128, 8], F32))
        s_idf = e(nc.sbuf_tensor("s_idf", [128, 128], F32))
        s_ones1 = e(nc.sbuf_tensor("s_ones1", [1, 128], F32))
        s_onesc = e(nc.sbuf_tensor("s_onesc", [128, 1], F32))
        s_one64 = e(nc.sbuf_tensor("s_one64", [128, 64], F32))
        s_nrm = e(nc.sbuf_tensor("s_nrm", [128, 64], F32))
        s_g8 = e(nc.sbuf_tensor("s_g8", [128, 8], F32))
        s_g64 = e(nc.sbuf_tensor("s_g64", [128, 64], F32))
        s_tmp64 = e(nc.sbuf_tensor("s_tmp64", [128, 64], F32))
        s_m = e(nc.sbuf_tensor("s_m", [128, 1], F32))
        s_eq = e(nc.sbuf_tensor("s_eq", [128, 8], F32))
        s_cum = e(nc.sbuf_tensor("s_cum", [128, 8], F32))
        s_p8 = e(nc.sbuf_tensor("s_p8", [128, 8], F32))
        s_sg = e(nc.sbuf_tensor("s_sg", [128, 8], F32))
        s_p8b = e(nc.sbuf_tensor("s_p8b", [128, 8], F32))
        s_p64 = e(nc.sbuf_tensor("s_p64", [128, 64], F32))
        s_p64T = e(nc.sbuf_tensor("s_p64T", [64, 128], F32))
        s_hh = e(nc.sbuf_tensor("s_hh", [128, 16], F32))
        s_t16 = e(nc.sbuf_tensor("s_t16", [128, 16], F32))
        s_mag = e(nc.sbuf_tensor("s_mag", [128, 1], F32))
        s_magrow = e(nc.sbuf_tensor("s_magrow", [1, 128], F32))
        s_mb = e(nc.sbuf_tensor("s_mb", [128, 128], F32))
        s_yacc = e(nc.sbuf_tensor("s_yacc", [128, 8 * 128], F32))
        s_zq = e(nc.sbuf_tensor("s_zq", [128, 128], F32))
        s_mu2 = e(nc.sbuf_tensor("s_mu2", [1, 128], F32))
        s_m22 = e(nc.sbuf_tensor("s_m22", [1, 128], F32))
        s_var2 = e(nc.sbuf_tensor("s_var2", [1, 128], F32))
        s_rstd2 = e(nc.sbuf_tensor("s_rstd2", [1, 128], F32))
        s_out = e(nc.sbuf_tensor("s_out", [128, 128], F32))
        p_stat = e(nc.psum_tensor("p_stat", [1, GRP], F32))
        p_bcast = e(nc.psum_tensor("p_bcast", [128, GRP], F32))
        p_c = e(nc.psum_tensor("p_c", [128, GRP], F32))
        p_c2 = e(nc.psum_tensor("p_c2", [2, GRP], F32))
        p_ts = e(nc.psum_tensor("p_ts", [128, 72], F32))
        p_g = e(nc.psum_tensor("p_g", [128, 128], F32))
        p_dj = e(nc.psum_tensor("p_dj", [128, 128], F32))
        p_sm = e(nc.psum_tensor("p_sm", [128, 128], F32))
        p = Prog(nc, st)

        dma(p, s_ones1[:], ones1, [], [s_ones1])
        dma(p, s_onesc[:], onesc, [], [s_onesc])
        dma(p, s_idf[:], identf, [], [s_idf])
        dma(p, s_gcol[:], gcol, [], [s_gcol])
        dma(p, s_bcol[:], bcol, [], [s_bcol])
        dma(p, s_spcat[:], spcat, [], [s_spcat])
        dma(p, s_dirs[:], dirs, [], [s_dirs])
        dma(p, s_cb1[:], cb1col, [], [s_cb1])
        dma(p, s_cb2[:], cb2col, [], [s_cb2])
        dma(p, s_sraw.ap().rearrange("p (kt e) -> p kt e", e=64),
            sigsrawT.rearrange("(kt p) e -> p kt e", p=128), [], [s_sraw])
        for kt in range(8):
            dma(p, s_wst[:], cw1[kt * 128:(kt + 1) * 128, :], [s_cw1], [s_wst])
            p.op("vector", lambda e, kt=kt: e.tensor_copy(
                out=s_cw1[:, kt * 256:(kt + 1) * 256], in_=s_wst[:]),
                reads=[s_wst], writes=[s_cw1])
        dma(p, s_wst[:, 0:4].rearrange("p (kt c) -> p kt c", c=2), cw2.rearrange("(kt p) c -> p kt c", p=128),
            [s_cw2], [s_wst])
        p.op("vector", lambda e: e.tensor_copy(out=s_cw2[:], in_=s_wst[:, 0:4]),
             reads=[s_wst], writes=[s_cw2])
        p.op("vector", lambda e: e.memset(s_one64[:], 1.0), reads=[], writes=[s_one64])

        p.op("vector", lambda e: e.tensor_single_scalar(out=s_q[:], in_=s_sraw[:], scalar=0.3, op=ALU.is_gt),
             reads=[s_sraw], writes=[s_q])
        p.op("vector", lambda e: e.tensor_single_scalar(out=s_cs[:, 0:64], in_=s_sraw[:, 0:64], scalar=-0.3, op=ALU.is_lt),
             reads=[s_sraw], writes=["qtmp"])
        p.op("vector", lambda e: e.engine_nop(), reads=["qtmp"], writes=[])
        for kt in range(8):
            p.op("vector", lambda e, kt=kt: e.tensor_single_scalar(
                out=s_tmp64[:], in_=s_sraw[:, kt * 64:(kt + 1) * 64], scalar=-0.3, op=ALU.is_lt),
                reads=[s_sraw], writes=[s_tmp64])
            p.op("vector", lambda e, kt=kt: e.tensor_sub(
                out=s_q[:, kt * 64:(kt + 1) * 64], in0=s_q[:, kt * 64:(kt + 1) * 64], in1=s_tmp64[:]),
                reads=[s_q, s_tmp64], writes=[s_q])
            qv = s_q.ap().rearrange("p (kt cl tpc) -> p kt cl tpc", kt=8, cl=8)
            p.op("vector", lambda e, kt=kt, qv=qv: e.tensor_reduce(
                out=s_t8[:], in_=qv[:, kt, :, :], axis=AX, op=ALU.add),
                reads=[s_q], writes=[s_t8])
            p.op("scalar", lambda e, kt=kt: e.activation(
                out=s_cs[:, kt * 8:(kt + 1) * 8], in_=s_t8[:], func=AF.Sign),
                reads=[s_t8], writes=[s_cs])
        dma(p, s_r1[:], temp, [], [s_r1])
        p.op("vector", lambda e: e.tensor_scalar_add(out=s_r1[:], in0=s_r1[:], scalar1=1e-6),
             reads=[s_r1], writes=[s_r1], small=True)
        p.op("vector", lambda e: e.reciprocal(out=s_r1[:], in_=s_r1[:]),
             reads=[s_r1], writes=[s_r1], small=True)
        p.op("tensor", lambda e: e.matmul(out=p_sm[:, 0:1], lhsT=s_ones1[:], rhs=s_r1[:],
                                          start=True, stop=True), reads=[s_ones1, s_r1], writes=[p_sm])
        p.op("scalar", lambda e: e.copy(out=s_rt[:], in_=p_sm[:, 0:1]), reads=[p_sm], writes=[s_rt])
        p.op("vector", lambda e: e.tensor_scalar_mul(out=s_nrt[:], in0=s_rt[:], scalar1=-1.0),
             reads=[s_rt], writes=[s_nrt], small=True)
        dma(p, s_r1[:], osc, [s_r1], [s_r1])
        p.op("tensor", lambda e: e.matmul(out=p_sm[:, 1:2], lhsT=s_ones1[:], rhs=s_r1[:],
                                          start=True, stop=True), reads=[s_ones1, s_r1], writes=[p_sm])
        p.op("scalar", lambda e: e.copy(out=s_osc[:], in_=p_sm[:, 1:2]), reads=[p_sm], writes=[s_osc])
        dma(p, s_mu2[:, 0:8], knots, ["kn_used"], [s_mu2])
        p.op("tensor", lambda e: e.matmul(out=p_sm[:, 8:16], lhsT=s_ones1[:], rhs=s_mu2[0:1, 0:8],
                                          start=True, stop=True), reads=[s_ones1, s_mu2], writes=[p_sm])
        p.op("scalar", lambda e: e.copy(out=s_kn[:], in_=p_sm[:, 8:16]), reads=[p_sm], writes=[s_kn, "kn_used"])
        p.op("vector", lambda e: e.tensor_sub(out=s_dk[:], in0=s_kn[:, 1:8], in1=s_kn[:, 0:7]),
             reads=[s_kn], writes=[s_dk])

        def calib(dst, src_is_psum, src, width):
            p.op("scalar", lambda e: e.activation(out=s_nrm[:, 0:width], in_=src, func=AF.Exp,
                                                  scale=s_nrt[:, 0:1]),
                 reads=[src_is_psum, s_nrt], writes=[s_nrm])
            p.op("vector", lambda e: e.tensor_scalar_add(out=s_nrm[:, 0:width], in0=s_nrm[:, 0:width],
                                                         scalar1=1.0),
                 reads=[s_nrm], writes=[s_nrm])
            p.op("vector", lambda e: e.reciprocal(out=s_nrm[:, 0:width], in_=s_nrm[:, 0:width]),
                 reads=[s_nrm], writes=[s_nrm], small=True)
            p.op("vector", lambda e: e.tensor_scalar(out=dst, in0=s_nrm[:, 0:width], scalar1=0.0,
                                                     scalar2=s_kn[:, 0:1], op0=ALU.mult, op1=ALU.add),
                 reads=[s_nrm, s_kn], writes=[dst.tensor.name])
            for i in range(7):
                p.op("vector", lambda e, i=i: e.tensor_scalar(
                    out=s_tmp64[:, 0:width], in0=s_nrm[:, 0:width], scalar1=7.0, scalar2=float(-i),
                    op0=ALU.mult, op1=ALU.add), reads=[s_nrm], writes=[s_tmp64])
                p.op("vector", lambda e: e.tensor_scalar(
                    out=s_tmp64[:, 0:width], in0=s_tmp64[:, 0:width], scalar1=1.0, scalar2=0.0,
                    op0=ALU.min, op1=ALU.max), reads=[s_tmp64], writes=[s_tmp64])
                p.op("vector", lambda e, i=i: e.scalar_tensor_tensor(
                    out=dst, in0=s_tmp64[:, 0:width], scalar=s_dk[:, i:i + 1], in1=dst,
                    op0=ALU.mult, op1=ALU.add), reads=[s_tmp64, s_dk, dst.tensor.name], writes=[dst.tensor.name])

        def argmax8(gates, out_p8):
            p.op("vector", lambda e, gates=gates: e.reduce_max(out=s_m[:], in_=gates, axis=AX, negate=False),
                 reads=[gates.tensor.name], writes=[s_m], small=True)
            p.op("vector", lambda e, gates=gates: e.tensor_scalar(
                out=s_eq[:], in0=gates, scalar1=s_m[:, 0:1], scalar2=None, op0=ALU.is_ge),
                 reads=[gates.tensor.name, s_m], writes=[s_eq])
            p.op("vector", lambda e: e.tensor_tensor_scan(
                out=s_cum[:], data0=s_one64[:, 0:8], data1=s_eq[:], initial=0.0,
                op0=ALU.mult, op1=ALU.add), reads=[s_one64, s_eq], writes=[s_cum])
            p.op("vector", lambda e: e.tensor_single_scalar(out=s_cum[:], in_=s_cum[:], scalar=1.0, op=ALU.is_le),
                 reads=[s_cum], writes=[s_cum])
            p.op("vector", lambda e, out_p8=out_p8: e.tensor_mul(out=out_p8, in0=s_eq[:], in1=s_cum[:]),
                 reads=[s_eq, s_cum], writes=[out_p8.tensor.name])

        for b in range(B_LOC):
            for g in range(NGRP):
                c0 = g * GRP
                for kt in range(8):
                    dma(p, s_y.ap()[:, kt * GRP:(kt + 1) * GRP],
                        y3T[b, kt * 128:(kt + 1) * 128, c0:c0 + GRP], [s_y], [s_y])
                sa_ = s_y.ap()
                for kt in range(8):
                    p.op("tensor", lambda e, kt=kt: e.matmul(
                        out=p_stat[:], lhsT=s_onesc[:], rhs=sa_[:, kt * GRP:(kt + 1) * GRP],
                        start=(kt == 0), stop=(kt == 7)), reads=[s_onesc, s_y], writes=[p_stat])
                p.op("scalar", lambda e: e.activation(out=s_mu[:], in_=p_stat[:], func=AF.Copy, scale=1.0 / D),
                     reads=[p_stat], writes=[s_mu], small=True)
                for kt in range(8):
                    p.op("scalar", lambda e, kt=kt: e.activation(
                        out=s_zsq[:], in_=sa_[:, kt * GRP:(kt + 1) * GRP], func=AF.Square),
                        reads=[s_y], writes=[s_zsq])
                    p.op("tensor", lambda e, kt=kt: e.matmul(
                        out=p_stat[:], lhsT=s_onesc[:], rhs=s_zsq[:],
                        start=(kt == 0), stop=(kt == 7)), reads=[s_onesc, s_zsq], writes=[p_stat])
                p.op("scalar", lambda e: e.activation(out=s_m2[:], in_=p_stat[:], func=AF.Copy, scale=1.0 / D),
                     reads=[p_stat], writes=[s_m2], small=True)
                p.op("vector", lambda e: e.tensor_mul(out=s_var[:], in0=s_mu[:], in1=s_mu[:]),
                     reads=[s_mu], writes=[s_var], small=True)
                p.op("vector", lambda e: e.tensor_sub(out=s_var[:], in0=s_m2[:], in1=s_var[:]),
                     reads=[s_m2, s_var], writes=[s_var], small=True)
                p.op("vector", lambda e: e.tensor_scalar_add(out=s_var[:], in0=s_var[:], scalar1=1e-5),
                     reads=[s_var], writes=[s_var], small=True)
                p.op("scalar", lambda e: e.activation(out=s_rstd[:], in_=s_var[:], func=AF.Ln),
                     reads=[s_var], writes=[s_rstd], small=True)
                p.op("scalar", lambda e: e.activation(out=s_rstd[:], in_=s_rstd[:], func=AF.Exp, scale=-0.5),
                     reads=[s_rstd], writes=[s_rstd], small=True)
                p.op("tensor", lambda e: e.matmul(out=p_bcast[:], lhsT=s_ones1[:], rhs=s_mu[:],
                                                  start=True, stop=True), reads=[s_ones1, s_mu], writes=[p_bcast])
                for kt in range(8):
                    p.op("vector", lambda e, kt=kt: e.tensor_sub(
                        out=s_xf.ap()[:, kt * GRP:(kt + 1) * GRP], in0=sa_[:, kt * GRP:(kt + 1) * GRP],
                        in1=p_bcast[:]), reads=[s_y, p_bcast], writes=[s_xf])
                p.op("tensor", lambda e: e.matmul(out=p_bcast[:], lhsT=s_ones1[:], rhs=s_rstd[:],
                                                  start=True, stop=True), reads=[s_ones1, s_rstd], writes=[p_bcast])
                for kt in range(8):
                    p.op("vector", lambda e, kt=kt: e.tensor_mul(
                        out=s_xf.ap()[:, kt * GRP:(kt + 1) * GRP], in0=s_xf.ap()[:, kt * GRP:(kt + 1) * GRP],
                        in1=p_bcast[:]), reads=[s_xf, p_bcast], writes=[s_xf])
                for kt in range(8):
                    p.op("vector", lambda e, kt=kt: e.tensor_copy(
                        out=s_xfb.ap()[:, kt * GRP:(kt + 1) * GRP], in_=s_xf.ap()[:, kt * GRP:(kt + 1) * GRP]),
                        reads=[s_xf], writes=[s_xfb])
                for i0 in range(2):
                    for kt in range(8):
                        p.op("tensor", lambda e, i0=i0, kt=kt: e.matmul(
                            out=p_c[:], lhsT=s_cw1[:, kt * 256 + i0 * 128: kt * 256 + i0 * 128 + 128],
                            rhs=s_xfb.ap()[:, kt * GRP:(kt + 1) * GRP],
                            start=(kt == 0), stop=(kt == 7)), reads=[s_cw1, s_xfb], writes=[p_c])
                    p.op("scalar", lambda e, i0=i0: e.activation(
                        out=s_hc.ap()[:, i0 * GRP:(i0 + 1) * GRP], in_=p_c[:], func=AF.Gelu,
                        bias=s_cb1[:, i0:i0 + 1]), reads=[p_c, s_cb1], writes=[s_hc])
                for kt2 in range(2):
                    p.op("tensor", lambda e, kt2=kt2: e.matmul(
                        out=p_c2[:], lhsT=s_cw2[:, kt2 * 2:(kt2 + 1) * 2],
                        rhs=s_hc.ap()[:, kt2 * GRP:(kt2 + 1) * GRP],
                        start=(kt2 == 0), stop=(kt2 == 1)), reads=[s_cw2, s_hc], writes=[p_c2])
                p.op("scalar", lambda e: e.activation(out=s_comp[:], in_=p_c2[:], func=AF.Tanh,
                                                      bias=s_cb2[:, 0:1]), reads=[p_c2, s_cb2], writes=[s_comp])
                for tt in range(4):
                    p.op("tensor", lambda e, tt=tt: e.transpose(
                        out=p_sm[:, 16:18], in_=s_comp[0:2, tt * 128:(tt + 1) * 128],
                        identity=s_idf[0:2, 0:2]), reads=[s_comp, s_idf], writes=[p_sm])
                    p.op("scalar", lambda e, tt=tt: e.copy(out=s_compT[:, tt * 2:(tt + 1) * 2],
                                                           in_=p_sm[:, 16:18]),
                         reads=[p_sm], writes=[s_compT])
                for tt in range(4):
                    for kt in range(8):
                        p.op("tensor", lambda e, kt=kt, tt=tt: e.matmul(
                            out=p_ts[:, 0:64], lhsT=s_xf.ap()[:, kt * GRP + tt * 128: kt * GRP + tt * 128 + 128],
                            rhs=s_q[:, kt * 64:(kt + 1) * 64],
                            start=(kt == 0), stop=(kt == 7)), reads=[s_xf, s_q], writes=[p_ts])
                    for kt in range(8):
                        p.op("tensor", lambda e, kt=kt, tt=tt: e.matmul(
                            out=p_ts[:, 64:72], lhsT=s_xf.ap()[:, kt * GRP + tt * 128: kt * GRP + tt * 128 + 128],
                            rhs=s_cs[:, kt * 8:(kt + 1) * 8],
                            start=(kt == 0), stop=(kt == 7)), reads=[s_xf, s_cs], writes=[p_ts])
                    p.op("vector", lambda e: e.tensor_copy(out=s_g8[:], in_=p_ts[:, 64:72]),
                         reads=[p_ts], writes=[s_g8])
                    p.op("vector", lambda e: e.tensor_copy(out=s_g64[:], in_=p_ts[:, 0:64]),
                         reads=[p_ts], writes=[s_g64])
                    argmax8(s_g8[:], s_p8[:])
                    for cl in range(8):
                        if cl == 0:
                            p.op("vector", lambda e: e.tensor_scalar(
                                out=s_sg[:], in0=s_g64[:, 0:8], scalar1=s_p8[:, 0:1], scalar2=None,
                                op0=ALU.mult), reads=[s_g64, s_p8], writes=[s_sg])
                        else:
                            p.op("vector", lambda e, cl=cl: e.scalar_tensor_tensor(
                                out=s_sg[:], in0=s_g64[:, cl * 8:(cl + 1) * 8], scalar=s_p8[:, cl:cl + 1],
                                in1=s_sg[:], op0=ALU.mult, op1=ALU.add),
                                reads=[s_g64, s_p8, s_sg], writes=[s_sg])
                    argmax8(s_sg[:], s_p8b[:])
                    for cl in range(8):
                        p.op("vector", lambda e, cl=cl: e.tensor_scalar(
                            out=s_p64[:, cl * 8:(cl + 1) * 8], in0=s_p8b[:], scalar1=s_p8[:, cl:cl + 1],
                            scalar2=None, op0=ALU.mult), reads=[s_p8b, s_p8], writes=[s_p64])
                    p.op("tensor", lambda e: e.transpose(out=p_g[0:64, 0:128], in_=s_p64[:], identity=s_idf[:]),
                         reads=[s_p64, s_idf], writes=[p_g])
                    p.op("scalar", lambda e: e.copy(out=s_p64T[:], in_=p_g[0:64, 0:128]),
                         reads=[p_g], writes=[s_p64T])
                    p.op("tensor", lambda e: e.matmul(out=p_g[:, 0:65], lhsT=s_p64T[:], rhs=s_spcat[:],
                                                      start=True, stop=True), reads=[s_p64T, s_spcat], writes=[p_g])
                    p.op("vector", lambda e, tt=tt: e.tensor_scalar(
                        out=s_t16[:], in0=p_g[:, 0:16], scalar1=s_compT[:, tt * 2:tt * 2 + 1],
                        scalar2=None, op0=ALU.mult), reads=[p_g, s_compT], writes=[s_t16])
                    p.op("vector", lambda e, tt=tt: e.scalar_tensor_tensor(
                        out=s_hh[:], in0=p_g[:, 16:32], scalar=s_compT[:, tt * 2 + 1:tt * 2 + 2],
                        in1=s_t16[:], op0=ALU.mult, op1=ALU.add),
                        reads=[p_g, s_compT, s_t16], writes=[s_hh])
                    p.op("vector", lambda e: e.tensor_add(out=s_hh[:], in0=s_hh[:], in1=p_g[:, 32:48]),
                         reads=[s_hh, p_g], writes=[s_hh])
                    p.op("vector", lambda e: e.tensor_scalar(out=s_hh[:], in0=s_hh[:], scalar1=0.0,
                                                             scalar2=None, op0=ALU.max),
                         reads=[s_hh], writes=[s_hh])
                    p.op("vector", lambda e: e.tensor_mul(out=s_t16[:], in0=s_hh[:], in1=p_g[:, 48:64]),
                         reads=[s_hh, p_g], writes=[s_t16])
                    p.op("vector", lambda e: e.reduce_sum(out=s_mag[:], in_=s_t16[:], axis=AX),
                         reads=[s_t16], writes=[s_mag], small=True)
                    p.op("vector", lambda e: e.tensor_add(out=s_mag[:], in0=s_mag[:], in1=p_g[:, 64:65]),
                         reads=[s_mag, p_g], writes=[s_mag], small=True)
                    p.op("vector", lambda e: e.tensor_scalar(out=s_mag[:], in0=s_mag[:], scalar1=s_osc[:, 0:1],
                                                             scalar2=None, op0=ALU.mult),
                         reads=[s_mag, s_osc], writes=[s_mag], small=True)
                    p.op("tensor", lambda e: e.transpose(out=p_c2[0:1, 0:128], in_=s_mag[:], identity=s_idf[:]),
                         reads=[s_mag, s_idf], writes=[p_c2])
                    p.op("scalar", lambda e: e.copy(out=s_magrow[:], in_=p_c2[0:1, 0:128]),
                         reads=[p_c2], writes=[s_magrow], small=True)
                    p.op("tensor", lambda e: e.matmul(out=p_sm[:], lhsT=s_ones1[:], rhs=s_magrow[:],
                                                      start=True, stop=True), reads=[s_ones1, s_magrow], writes=[p_sm])
                    p.op("scalar", lambda e: e.copy(out=s_mb[:], in_=p_sm[:]), reads=[p_sm], writes=[s_mb])
                    for j in range(8):
                        p.op("tensor", lambda e, j=j: e.matmul(
                            out=p_dj[:], lhsT=s_dirs[:, j * 128:(j + 1) * 128], rhs=s_p64T[:],
                            start=True, stop=True), reads=[s_dirs, s_p64T], writes=[p_dj])
                        p.op("vector", lambda e, j=j, tt=tt: e.tensor_mul(
                            out=s_zq[:], in0=p_dj[:], in1=s_mb[:]), reads=[p_dj, s_mb], writes=[s_zq])
                        p.op("vector", lambda e, j=j, tt=tt: e.tensor_add(
                            out=s_yacc.ap()[:, j * 128:(j + 1) * 128], in0=s_zq[:],
                            in1=s_xf.ap()[:, j * GRP + tt * 128: j * GRP + tt * 128 + 128]),
                            reads=[s_zq, s_xf], writes=[s_yacc])
                    ya = s_yacc.ap()
                    for kt in range(8):
                        p.op("tensor", lambda e, kt=kt: e.matmul(
                            out=p_stat[0:1, 0:128], lhsT=s_onesc[:], rhs=ya[:, kt * 128:(kt + 1) * 128],
                            start=(kt == 0), stop=(kt == 7)), reads=[s_onesc, s_yacc], writes=[p_stat])
                    p.op("scalar", lambda e: e.activation(out=s_mu2[:], in_=p_stat[0:1, 0:128], func=AF.Copy, scale=1.0 / D),
                         reads=[p_stat], writes=[s_mu2], small=True)
                    for kt in range(8):
                        p.op("scalar", lambda e, kt=kt: e.activation(
                            out=s_zq[:], in_=ya[:, kt * 128:(kt + 1) * 128], func=AF.Square),
                            reads=[s_yacc], writes=[s_zq])
                        p.op("tensor", lambda e, kt=kt: e.matmul(
                            out=p_stat[0:1, 0:128], lhsT=s_onesc[:], rhs=s_zq[:],
                            start=(kt == 0), stop=(kt == 7)), reads=[s_onesc, s_zq], writes=[p_stat])
                    p.op("scalar", lambda e: e.activation(out=s_m22[:], in_=p_stat[0:1, 0:128], func=AF.Copy, scale=1.0 / D),
                         reads=[p_stat], writes=[s_m22], small=True)
                    p.op("vector", lambda e: e.tensor_mul(out=s_var2[:], in0=s_mu2[:], in1=s_mu2[:]),
                         reads=[s_mu2], writes=[s_var2], small=True)
                    p.op("vector", lambda e: e.tensor_sub(out=s_var2[:], in0=s_m22[:], in1=s_var2[:]),
                         reads=[s_m22, s_var2], writes=[s_var2], small=True)
                    p.op("vector", lambda e: e.tensor_scalar_add(out=s_var2[:], in0=s_var2[:], scalar1=1e-5),
                         reads=[s_var2], writes=[s_var2], small=True)
                    p.op("scalar", lambda e: e.activation(out=s_rstd2[:], in_=s_var2[:], func=AF.Ln),
                         reads=[s_var2], writes=[s_rstd2], small=True)
                    p.op("scalar", lambda e: e.activation(out=s_rstd2[:], in_=s_rstd2[:], func=AF.Exp, scale=-0.5),
                         reads=[s_rstd2], writes=[s_rstd2], small=True)
                    p.op("tensor", lambda e: e.matmul(out=p_bcast[:, 0:128], lhsT=s_ones1[:], rhs=s_mu2[:],
                                                      start=True, stop=True), reads=[s_ones1, s_mu2], writes=[p_bcast])
                    for kt in range(8):
                        p.op("vector", lambda e, kt=kt: e.tensor_sub(
                            out=ya[:, kt * 128:(kt + 1) * 128], in0=ya[:, kt * 128:(kt + 1) * 128],
                            in1=p_bcast[:, 0:128]), reads=[s_yacc, p_bcast], writes=[s_yacc])
                    p.op("tensor", lambda e: e.matmul(out=p_bcast[:, 0:128], lhsT=s_ones1[:], rhs=s_rstd2[:],
                                                      start=True, stop=True), reads=[s_ones1, s_rstd2], writes=[p_bcast])
                    for kt in range(8):
                        p.op("vector", lambda e, kt=kt: e.tensor_mul(
                            out=ya[:, kt * 128:(kt + 1) * 128], in0=ya[:, kt * 128:(kt + 1) * 128],
                            in1=p_bcast[:, 0:128]), reads=[s_yacc, p_bcast], writes=[s_yacc])
                    for kt in range(8):
                        p.op("vector", lambda e, kt=kt: e.tensor_scalar(
                            out=ya[:, kt * 128:(kt + 1) * 128], in0=ya[:, kt * 128:(kt + 1) * 128],
                            scalar1=s_gcol[:, kt:kt + 1], scalar2=s_bcol[:, kt:kt + 1],
                            op0=ALU.mult, op1=ALU.add), reads=[s_yacc, s_gcol, s_bcol], writes=[s_yacc])
                        p.op("vector", lambda e, kt=kt, tt=tt: e.tensor_add(
                            out=s_out[:], in0=ya[:, kt * 128:(kt + 1) * 128],
                            in1=s_y.ap()[:, kt * GRP + tt * 128: kt * GRP + tt * 128 + 128]),
                            reads=[s_yacc, s_y], writes=[s_out])
                        p.op("sync", lambda e, b=b, kt=kt, c0=c0, tt=tt: e.dma_start(
                            out=outT[b, kt * 128:(kt + 1) * 128, c0 + tt * 128:c0 + tt * 128 + 128],
                            in_=s_out[:]), reads=[s_out], writes=["o_w"], dma=True)
        p.finalize()
    return nc
